# version 37
# baseline (speedup 1.0000x reference)
import numpy as np
from contextlib import ExitStack
import concourse.bass as bass
import concourse.mybir as mybir
from concourse.bass_utils import run_bass_kernel_spmd

F32 = mybir.dt.float32
BF16 = mybir.dt.bfloat16
AF = mybir.ActivationFunctionType
ALU = mybir.AluOpType

NCORES = 8
D, DH = 1024, 4096
NCH, HCH = 8, 32
SEQ = 4096
SPC = 2
GT = 512
TC = GT // 128
NGS = SEQ // GT
EPS = 1e-5
WTELEMS = 12288

WCFG = {
    "w1": (8, 256, 16), "w2": (8, 256, 16),
    "h1": (32, 128, 32), "h2": (32, 128, 32),
    "w3": (32, 128, 8),
    "kv": (8, 512, 4), "wq": (8, 512, 2), "wp": (8, 512, 2),
}
WQUEUE = {"w1": "sp", "w2": "sp", "h1": "sp", "kv": "sp", "wq": "sp", "wp": "sp",
          "h2": "pool", "w3": "pool"}

DEBUG_X1 = False


class Sem:
    def __init__(self, h):
        self.h = h
        self.n = 0

    def bump(self, k):
        self.n += k
        return self.n


class Banks:
    def __init__(self, tiles, store):
        self.tiles = tiles
        self.store = store
        self.i = 0
        self.width = 3

    def next(self):
        t = self.tiles[self.i % self.width]
        self.i += 1
        return t

    def set_width(self, w):
        self.width = w
        self.i = 0


class WTRing:
    def __init__(self, total):
        self.total = total
        self.pos = 0
        self.segs = []

    def alloc(self, n):
        if self.pos + n > self.total:
            self.pos = 0
        off = self.pos
        self.pos += n
        deps = [s[2] for s in self.segs if s[0] < off + n and s[1] > off and s[2] is not None]
        self.segs = [s for s in self.segs if not (s[0] >= off and s[1] <= off + n)]
        seg = [off, off + n, None]
        self.segs.append(seg)
        return off, deps, seg


class MK:
    def __init__(self):
        nc = bass.Bass("TRN2", target_bir_lowering=False)
        self.nc = nc
        self.ops = {e: [] for e in ("pe", "dve", "act", "sp", "pool")}
        self.waited = {}

        dt = nc.dram_tensor
        self.x0 = dt("x0", [D, SPC * SEQ], F32, kind="ExternalInput").ap()
        self.y = dt("y", [D, SPC * SEQ], F32, kind="ExternalOutput").ap()
        x1kind = "ExternalOutput" if DEBUG_X1 else "Internal"
        self.x1s = dt("x1s", [D, SPC * SEQ], F32, kind=x1kind).ap()
        self.qs = dt("qs", [D, SPC * SEQ], BF16, kind="Internal").ap()
        self.wf32 = {}
        self.wb16 = {}
        for name, (KC, FBW, NFB) in WCFG.items():
            self.wf32[name] = dt(name + "t", [NFB, 128, KC * FBW], F32, kind="ExternalInput").ap()
            self.wb16[name] = dt(name + "b", [NFB, 128, KC * FBW], BF16, kind="Internal").ap()
        self.bkv_f = dt("bkv", [1, 2048], F32, kind="ExternalInput").ap()
        self.bkv_b = dt("bkvb", [1, 2048], BF16, kind="Internal").ap()
        self.cols_in = {}
        for cname, w in [("s1c", 32), ("s2c", 32), ("sh1c", 32), ("sh2c", 32),
                         ("s3c", NCH), ("gamc", NCH), ("betc", NCH),
                         ("bqc", NCH), ("bpc", NCH)]:
            self.cols_in[cname] = dt(cname, [128, w], F32, kind="ExternalInput").ap()

        self.es = ExitStack()

    def build(self):
        nc = self.nc
        es = self.es
        sb = lambda name, shape, dtyp: es.enter_context(nc.sbuf_tensor(name, shape, dtyp))
        self.RESP = [sb("RES0", [128, NCH * GT], F32), sb("RES1", [128, NCH * GT], F32)]
        self.XB = sb("XB", [128, NCH * GT], BF16)
        self.QB = sb("QB", [128, NCH * GT], BF16)
        self.PO = sb("PO", [128, NCH * GT], BF16)
        self.HIN = sb("HIN", [128, HCH * GT], BF16)
        self.HOUT = sb("HOUT", [128, HCH * GT], BF16)
        self.WT = sb("WT", [128, WTELEMS], BF16)
        self.WPT = sb("WPT", [128, 8192], BF16)
        self.KT = sb("KT", [128, 2048], BF16)
        self.VT = sb("VT", [128, 4096], BF16)
        self.KVE = sb("KVE", [128, 2 * 512], BF16)
        self.K1M2 = sb("K1M2", [128, 2 * 8 * 33], BF16)
        self.REC33 = sb("REC33", [33, GT], BF16)
        self.REC = sb("REC", [128, GT], BF16)
        self.SCR = [sb(f"SCR{i}", [128, GT], F32) for i in range(4)]
        self.scr_free = [None] * 4
        self.scr_i = 0
        self.SUMP = [(sb("SUMA", [128, GT], F32), sb("SQA", [128, GT], F32)),
                     (sb("SUMB", [128, GT], F32), sb("SQB", [128, GT], F32))]
        self.ROWS = sb("ROWS", [33, 3 * GT], F32)
        self.ROWSB = sb("ROWSB", [33, 2 * GT], BF16)
        self.BKV = sb("BKV", [1, 2048], BF16)
        self.ONER = sb("ONER", [33, 128], F32)
        self.ONE33 = sb("ONE33", [33, 64], BF16)
        self.ONEB = sb("ONEB", [33, 128], BF16)
        self.ONEC = sb("ONEC", [128, 1], F32)
        self.ONECB = sb("ONECB", [128, 1], BF16)
        self.EPSC = sb("EPSC", [128, 1], F32)
        self.COLS = {}
        for cname in self.cols_in:
            w = 32 if cname.startswith("s") and cname not in ("s3c",) else NCH
            self.COLS[cname] = sb("C_" + cname, [128, w], F32)
        self.S3H = sb("S3H", [128, NCH], F32)
        self.BQ1 = sb("BQ1", [128, NCH], F32)

        ps = lambda name: es.enter_context(nc.psum_tensor(name, [128, 512], F32))
        mA, mB, mC = ps("mA"), ps("mB"), ps("mC")
        self.KV0 = ps("kv0")
        self.KV1 = ps("kv1")
        self.K1P = ps("k1p")
        self.BCA = ps("bcA")
        self.BCB = ps("bcB")
        self.bfree = {}
        self.banks = Banks([mA, mB, mC, self.BCA, self.BCB], self.bfree)

        mksem = lambda n: Sem(es.enter_context(nc.semaphore(n)))
        self.sPE = mksem("sPE")
        self.sDVE = mksem("sDVE")
        self.sACT = mksem("sACT")
        self.qsems = {q: [mksem(f"{q}{i}") for i in range(3)] for q in ("sp", "pool", "act")}
        self.qcnt = {q: 0 for q in ("sp", "pool", "act")}
        self.own = {"pe": self.sPE, "dve": self.sDVE, "act": self.sACT}
        self.esem = {"pe": self.sPE, "dve": self.sDVE, "act": self.sACT}

        self.ring = WTRing(WTELEMS)

        self.hin_free = None
        self.hout_free = None
        self.kv_bank_free = None
        self.rows_free = None
        self.qb_store = None
        self.k1p_free = None
        self.rec_free = None
        self.po_free = None

        self.res_free = [[], []]
        self.kv_pend = None
        self.wp_free = None
        self.wp_ld = [None, None]
        self.qp2_free = None
        self.pre2 = None
        self.vt_free = None
        self.sum_free = [None, None]
        self.stats_dep_p = [None, None]

        self.GROUPS = ([("p1", s, g) for s in range(SPC) for g in range(NGS)]
                       + [("p2", s, g) for s in range(SPC) for g in range(NGS)])
        self.pre = {}
        self.qperm_dep = None
        self.emit_prefetch(0)
        self.preamble()
        self.emit_lnpre(0)
        for i, (kind, s, g) in enumerate(self.GROUPS):
            if kind == "p1":
                self.p1_main(i, s, g)
                if g == NGS - 1:
                    self.kv_evac(s)
                    if s == SPC - 1:
                        self.bfree[id(self.KV0)] = self.kv_bank_free
                        self.bfree[id(self.KV1)] = self.kv_bank_free
                        self.emit_lnpre(i + 1)
                        self.p2_island(i + 1)
                        self.p2_island_b()
            else:
                self.p2_main(i, s, g)
        for s_ in self.qsems["act"]:
            self.emit("act", lambda e, h=s_.h, v=s_.n: e.wait_ge(h, v))

        with nc.Block() as blk:
            blk.tensor(lambda e: [f(e) for f in self.ops["pe"]] and None)
            blk.vector(lambda e: [f(e) for f in self.ops["dve"]] and None)
            blk.scalar(lambda e: [f(e) for f in self.ops["act"]] and None)
            blk.sync(lambda e: [f(e) for f in self.ops["sp"]] and None)
            blk.gpsimd(lambda e: [f(e) for f in self.ops["pool"]] and None)
        self.es.close()
        return nc

    def emit(self, eng, fn):
        self.ops[eng].append(fn)

    def wait(self, eng, dep):
        if dep is None:
            return
        sem, val = dep
        if val <= 0 or sem is self.own.get(eng):
            return
        key = (eng, id(sem))
        if self.waited.get(key, 0) >= val:
            return
        self.waited[key] = val
        self.emit(eng, lambda e, h=sem.h, v=val: e.wait_ge(h, v))

    def op(self, eng, build, deps=(), inc=None):
        for d in deps:
            self.wait(eng, d)
        if inc is not None:
            sem, k = inc
            val = sem.bump(k)

            def run(e, b=build, h=sem.h, kk=k):
                b(e).then_inc(h, kk)
            self.emit(eng, run)
            return (sem, val)
        else:
            self.emit(eng, lambda e, b=build: b(e))
            return None

    def dma(self, q, dst, src, deps=()):
        sem = self.qsems[q][self.qcnt[q] % 3]
        self.qcnt[q] += 1
        self.wait(q, (sem, sem.n))
        return self.op(q, lambda e, d=dst, s=src: e.dma_start(d, s), deps, inc=(sem, 16))

    def barrier(self):
        snap = {e: s.n for e, s in self.esem.items()}
        for e in self.esem:
            for o, v in snap.items():
                if o != e:
                    self.wait(e, (self.esem[o], v))

    def scr(self):
        i = self.scr_i % 4
        self.scr_i += 1
        tile = self.SCR[i]
        dep = self.scr_free[i]

        def setter(d, i=i):
            self.scr_free[i] = d
        return tile, dep, setter

    def preamble(self):
        nc = self.nc
        for cname, ap in self.cols_in.items():
            self.dma("act", self.COLS[cname][:, :], ap)
        crit = [(s_, s_.n) for s_ in self.qsems["act"]]
        self.conv = {}
        for d_ in crit:
            self.wait("pool", d_)
        self.conv_bkv = self.dma("pool", self.bkv_b[:, :], self.bkv_f[:, :])
        pieces = [("w1", 0, 1), ("w2", 0, 1), ("w1", 1, 2), ("w2", 1, 2),
                  ("w1", 2, 3), ("w2", 2, 3), ("w1", 3, 4), ("w2", 3, 4),
                  ("h1", 0, 2), ("h2", 0, 2), ("w1", 4, 7), ("w2", 4, 7),
                  ("h1", 2, 4), ("h2", 2, 4), ("w1", 7, 10), ("w2", 7, 10),
                  ("h1", 4, 6), ("h2", 4, 6), ("w1", 10, 13), ("w2", 10, 13),
                  ("h1", 6, 8), ("h2", 6, 8), ("w1", 13, 16), ("w2", 13, 16),
                  ("h1", 8, 10), ("h2", 8, 10), ("w3", 0, 4),
                  ("h1", 10, 14), ("h2", 10, 14), ("w3", 4, 8),
                  ("kv", 0, 4), ("wq", 0, 2), ("wp", 0, 2),
                  ("h1", 14, 22), ("h2", 14, 22), ("h1", 22, 32), ("h2", 22, 32)]
        for name, f0, f1 in pieces:
            d = self.dma("pool", self.wb16[name][f0:f1], self.wf32[name][f0:f1])
            for fb in range(f0, f1):
                self.conv[(name, fb)] = d
        dl = self.dma("act", self.BKV[:, :], self.bkv_b[:, :], deps=[self.conv_bkv])
        cdeps = [(s_, s_.n) for s_ in self.qsems["act"]]
        for cdep in cdeps:
            self.wait("pe", cdep)
            self.wait("dve", cdep)
        self.op("act", lambda e: e.activation(self.S3H[:, :], self.COLS["s3c"][:, :], AF.Copy, scale=0.5),
                deps=cdeps, inc=(self.sACT, 1))
        self.bq1_dep = self.op("act", lambda e: e.activation(self.BQ1[:, :], self.COLS["bqc"][:, :], AF.Copy, bias=1.0),
                               deps=cdeps, inc=(self.sACT, 1))
        self.op("dve", lambda e: e.memset(self.ONER[:, :], 1.0))
        self.op("dve", lambda e: e.memset(self.ONE33[:, :], 1.0))
        self.op("dve", lambda e: e.memset(self.ONEB[:, :], 1.0))
        self.op("dve", lambda e: e.memset(self.ONEC[:, :], 1.0))
        self.op("dve", lambda e: e.memset(self.EPSC[:, :], EPS))
        self.ones_dep = self.op("dve", lambda e: e.memset(self.ONECB[:, :], 1.0), inc=(self.sDVE, 1))
        self.barrier()

    def stats_chunk(self, pi, src, c, first, src_dep):
        SUMA, SQA = self.SUMP[pi]
        s = src[:, c * GT:(c + 1) * GT]
        scr, wdep, sset = self.scr()
        sq = self.op("act", lambda e, s=s, o=scr: e.activation(o[:, :], s, AF.Square),
                     deps=[src_dep, wdep], inc=(self.sACT, 1))
        if first:
            d1 = self.op("dve", lambda e, s=s, SUMA=SUMA: e.tensor_copy(SUMA[:, :], s),
                         deps=[src_dep, self.sum_free[pi]], inc=(self.sDVE, 1))
            d2 = self.op("dve", lambda e, o=scr, SQA=SQA: e.tensor_copy(SQA[:, :], o[:, :]),
                         deps=[sq], inc=(self.sDVE, 1))
        else:
            d1 = self.op("dve", lambda e, s=s, SUMA=SUMA: e.tensor_add(SUMA[:, :], SUMA[:, :], s),
                         deps=[src_dep], inc=(self.sDVE, 1))
            d2 = self.op("dve", lambda e, o=scr, SQA=SQA: e.tensor_add(SQA[:, :], SQA[:, :], o[:, :]),
                         deps=[sq], inc=(self.sDVE, 1))
        sset(d2)
        self.stats_dep_p[pi] = (self.sDVE, self.sDVE.n)

    def ln_finish(self, pi, rb_bank, ab_bank, bf=False, defer=False):
        SUMA, SQA = self.SUMP[pi]
        R = self.ROWS
        r_mean = R[32:33, 0:GT]
        r_sc = r_var = R[32:33, GT:2 * GT]
        bfree = self.bfree.get(id(rb_bank))
        bfree2 = self.bfree.get(id(ab_bank))
        d_sum = self.op("pe", lambda e, b=rb_bank, SUMA=SUMA: e.matmul(b[0:1, :], self.ONEC[:, :], SUMA[:, :], start=True, stop=True),
                        deps=[self.stats_dep_p[pi], bfree, self.ones_dep], inc=(self.sPE, 1))
        d_sq = self.op("pe", lambda e, b=rb_bank, SQA=SQA: e.matmul(b[32:33, :], self.ONEC[:, :], SQA[:, :], start=True, stop=True),
                       deps=[], inc=(self.sPE, 1))
        if bf:
            out_r = self.ROWSB[32:33, 0:GT]
            out_a = self.ROWSB[32:33, GT:2 * GT]
            on_r = on_a = self.ONEB[32:33, 0:128]
        else:
            out_r = r_var
            out_a = R[32:33, 2 * GT:3 * GT]
            on_r = on_a = self.ONER[32:33, 0:128]
        self.op("dve", lambda e, b=rb_bank, o=r_mean: e.tensor_scalar(o, b[0:1, :], 1.0 / D, None, ALU.mult),
                deps=[d_sq, self.rows_free])
        self.op("dve", lambda e, o=r_sc, i=r_mean: e.tensor_tensor(o, i, i, ALU.mult), deps=[])
        d1 = self.op("dve", lambda e, b=rb_bank, o=r_var, i=r_sc: e.scalar_tensor_tensor(
            o, b[32:33, :], 1.0 / D, i, ALU.mult, ALU.subtract), deps=[], inc=(self.sDVE, 1))
        self.op("act", lambda e, o=r_var, i=r_var: e.activation(o, i, AF.Ln, bias=self.EPSC[32:33, 0:1]),
                deps=[d1])
        dsq = self.op("act", lambda e, o=out_r, i=r_var: e.activation(o, i, AF.Exp, scale=-0.5),
                      deps=[], inc=(self.sACT, 1))
        drow = self.op("dve", lambda e, o=out_a, m=r_mean, r=out_r: e.scalar_tensor_tensor(o, m, -1.0, r, ALU.mult, ALU.mult),
                       deps=[dsq], inc=(self.sDVE, 1))

        def part2():
            rb_dep = self.op("pe", lambda e, b=rb_bank, r=out_r, o=on_r: e.matmul(b[:, :], o, r, start=True, stop=True),
                             deps=[dsq, d1], inc=(self.sPE, 1))
            ab_dep = self.op("pe", lambda e, b=ab_bank, a=out_a, o=on_a: e.matmul(b[:, :], o, a, start=True, stop=True),
                             deps=[drow, bfree2], inc=(self.sPE, 1))
            self.rows_free = ab_dep
            return rb_dep, ab_dep
        self.sum_free[pi] = d_sq
        if defer:
            return part2
        return part2()

    def ln_apply(self, src, rb_bank, ab_bank, rb_dep, ab_dep, dst_dep_extra=None, dst=None):
        last = self.ln_apply_cd(src, rb_bank, ab_bank, rb_dep, ab_dep, dst_dep_extra, dst)[-1]
        return last

    def ln_apply_chunk(self, src, rb_bank, ab_bank, rb_dep, ab_dep, c, dst, dst_dep_extra=None):
        s = src[:, c * GT:(c + 1) * GT]
        scr, wdep, sset = self.scr()
        self.op("dve", lambda e, s=s, o=scr, b=rb_bank: e.tensor_mul(o[:, :], s, b[:, :]),
                deps=[rb_dep, wdep, dst_dep_extra])
        last = self.op("dve", lambda e, o=scr, b=ab_bank, c=c, dst=dst: e.tensor_add(
            dst[:, c * GT:(c + 1) * GT], o[:, :], b[:, :]), deps=[ab_dep], inc=(self.sDVE, 1))
        sset(last)
        return last

    def ln_apply_cd(self, src, rb_bank, ab_bank, rb_dep, ab_dep, dst_dep_extra=None, dst=None):
        if dst is None:
            dst = self.XB
        cds = []
        for c in range(NCH):
            s = src[:, c * GT:(c + 1) * GT]
            scr, wdep, sset = self.scr()
            self.op("dve", lambda e, s=s, o=scr, b=rb_bank: e.tensor_mul(o[:, :], s, b[:, :]),
                    deps=[rb_dep, wdep, dst_dep_extra])
            last = self.op("dve", lambda e, o=scr, b=ab_bank, c=c, dst=dst: e.tensor_add(
                dst[:, c * GT:(c + 1) * GT], o[:, :], b[:, :]), deps=[ab_dep], inc=(self.sDVE, 1))
            sset(last)
            cds.append(last)
        self.bfree[id(rb_bank)] = cds[-1]
        self.bfree[id(ab_bank)] = cds[-1]
        return cds

    def load_wtile(self, name, fb):
        KC, FBW, NFB = WCFG[name]
        n = KC * FBW
        off, wdeps, seg = self.ring.alloc(n)
        q = WQUEUE[name]
        deps = list(wdeps) + [self.conv[(name, fb)]]
        d = self.dma(q, self.WT[:, off:off + n], self.wb16[name][fb], deps=deps)
        return off, d, seg

    def mm_group(self, off, ldep, KC, FBW, m, rhs_sb, rhs_dep, extra_mm=None, extra_deps=(), arena=None, mid_deps=None):
        wt = self.WT if arena is None else arena
        bank = self.banks.next()
        deps = [self.bfree.get(id(bank)), ldep, rhs_dep] + list(extra_deps)
        for k in range(KC):
            lh = wt[:, off + k * FBW + m * 128: off + k * FBW + (m + 1) * 128]
            rh = rhs_sb[:, k * GT:(k + 1) * GT]
            is_last = (k == KC - 1) and extra_mm is None
            kdeps = deps if k == 0 else ([mid_deps[k]] if mid_deps and k in mid_deps else [])
            d = self.op("pe", lambda e, b=bank, l=lh, r=rh, st=(k == 0), sp=is_last: e.matmul(
                b[:, :], l, r, start=st, stop=sp),
                deps=kdeps, inc=(self.sPE, 1) if is_last else None)
        if extra_mm is not None:
            d = self.op("pe", extra_mm(bank), deps=[], inc=(self.sPE, 1))
        return bank, d

    def stage_w1w2(self, rhs_sb, xb_dep, hooks=None):
        KC, FBW, NFB = WCFG["w1"]
        MPF = FBW // 128
        for fb in range(NFB):
            if hooks and fb in hooks:
                hooks[fb]()
            off1, l1, seg1 = self.load_wtile("w1", fb)
            off2, l2, seg2 = self.load_wtile("w2", fb)
            for m in range(MPF):
                gm = fb * MPF + m
                b1, d1 = self.mm_group(off1, l1, KC, FBW, m, rhs_sb, xb_dep)
                b2, d2 = self.mm_group(off2, l2, KC, FBW, m, rhs_sb, xb_dep)
                scr, wdep, sset = self.scr()
                da = self.op("act", lambda e, b=b1, o=scr, gm=gm: e.activation(
                    o[:, :], b[:, :], AF.Silu, scale=self.COLS["s1c"][:, gm:gm + 1]),
                    deps=[d1, wdep], inc=(self.sACT, 1))
                self.bfree[id(b1)] = da
                dd = self.op("dve", lambda e, b=b2, o=scr, gm=gm: e.scalar_tensor_tensor(
                    self.HIN[:, gm * GT:(gm + 1) * GT], b[:, :], self.COLS["s2c"][:, gm:gm + 1],
                    o[:, :], ALU.mult, ALU.mult),
                    deps=[d2, da, self.hin_free], inc=(self.sDVE, 1))
                self.bfree[id(b2)] = dd
                sset(dd)
            seg1[2] = (self.sPE, self.sPE.n)
            seg2[2] = (self.sPE, self.sPE.n)
        return (self.sDVE, self.sDVE.n)

    def stage_h(self, name, src, src_dep, dst, dst_free, scol, hooks=None):
        KC, FBW, NFB = WCFG[name]
        for fb in range(NFB):
            off, ld, seg = self.load_wtile(name, fb)
            for m in range(FBW // 128):
                gm = fb * (FBW // 128) + m
                bank, d = self.mm_group(off, ld, KC, FBW, m, src, src_dep)
                da = self.op("act", lambda e, b=bank, gm=gm, dst=dst: e.activation(
                    dst[:, gm * GT:(gm + 1) * GT], b[:, :], AF.Gelu, scale=self.COLS[scol][:, gm:gm + 1]),
                    deps=[d, dst_free], inc=(self.sACT, 1))
                self.bfree[id(bank)] = da
            seg[2] = (self.sPE, self.sPE.n)
            if hooks and fb in hooks:
                hooks[fb]()
        return (self.sACT, self.sACT.n), (self.sPE, self.sPE.n)

    def stage_w3(self, RES, pi, src, src_dep, with_stats, hooks=None):
        KC, FBW, NFB = WCFG["w3"]
        for fb in range(NFB):
            off, ld, seg = self.load_wtile("w3", fb)
            for m in range(FBW // 128):
                c = fb * (FBW // 128) + m
                bank, d = self.mm_group(off, ld, KC, FBW, m, src, src_dep)
                dd = self.op("dve", lambda e, b=bank, c=c, RES=RES: e.scalar_tensor_tensor(
                    RES[:, c * GT:(c + 1) * GT], b[:, :], self.S3H[:, c:c + 1],
                    RES[:, c * GT:(c + 1) * GT], ALU.mult, ALU.add),
                    deps=[d], inc=(self.sDVE, 1))
                self.bfree[id(bank)] = dd
                if with_stats:
                    self.stats_chunk(pi, RES, c, c == 0, dd)
            seg[2] = (self.sPE, self.sPE.n)
            if hooks and fb in hooks:
                hooks[fb]()
        return (self.sDVE, self.sDVE.n)

    def emit_prefetch(self, j):
        if j >= len(self.GROUPS):
            return
        kind, s, g = self.GROUPS[j]
        pi = j % 2
        t0 = s * SEQ + g * GT
        RES = self.RESP[pi]
        src = self.x0 if kind == "p1" else self.x1s
        for c in range(NCH):
            ld = self.dma("act", RES[:, c * GT:(c + 1) * GT],
                          src[c * 128:(c + 1) * 128, t0:t0 + GT],
                          deps=self.res_free[pi])
            self.stats_chunk(pi, RES, c, c == 0, ld)
        self.res_free[pi] = []
        if kind == "p2" and j > 0 and self.GROUPS[j - 1][0] == "p2":
            self.pre_qslab = self.dma("act", self.QB[:, :],
                                      self.qs[g * 128:(g + 1) * 128, s * SEQ:(s + 1) * SEQ],
                                      deps=[self.qperm_dep])

    def emit_lnpre(self, j):
        if j >= len(self.GROUPS):
            return
        kind, s, g = self.GROUPS[j]
        pi = j % 2
        if kind == "p1":
            rb, ab = self.ln_finish(pi, self.BCA, self.BCB, bf=True)
            xb = self.ln_apply(self.RESP[pi], self.BCA, self.BCB, rb, ab,
                               dst_dep_extra=self.po_free, dst=self.PO)
            self.pre = {"xb": xb}
        else:
            rb, ab = self.ln_finish(pi, self.KV0, self.KV1, bf=False)
            if self.GROUPS[j - 1][0] == "p2":
                qslab = self.pre_qslab
            else:
                qslab = self.dma("act", self.QB[:, :],
                                 self.qs[g * 128:(g + 1) * 128, s * SEQ:(s + 1) * SEQ],
                                 deps=[(self.sPE, self.sPE.n), (self.sDVE, self.sDVE.n)])
            self.pre = {"rb1": rb, "ab1": ab, "qslab": qslab}

    def p1_main(self, i, s, g):
        t0 = s * SEQ + g * GT
        pi = i % 2
        RES = self.RESP[pi]
        xb_dep = self.pre["xb"]
        self.banks.set_width(5)
        hin_dep = self.stage_w1w2(self.PO, xb_dep)
        self.po_free = (self.sPE, self.sPE.n)
        self.emit_prefetch(i + 1)
        hout_dep, hin_read = self.stage_h("h1", self.HIN, hin_dep, self.HOUT, self.hout_free, "sh1c")
        hin2_dep, hout_read = self.stage_h("h2", self.HOUT, hout_dep, self.HIN, hin_read, "sh2c")
        self.hout_free = hout_read
        x1_dep = self.stage_w3(RES, pi, self.HIN, hin2_dep, with_stats=True)
        self.hin_free = (self.sPE, self.sPE.n)
        self.banks.set_width(3)
        st = self.dma("act", self.x1s[:, t0:t0 + GT].rearrange("(c p) t -> p c t", p=128),
                      RES[:, :].rearrange("p (c t) -> p c t", c=NCH), deps=[x1_dep])
        f1 = self.ln_finish(pi, self.BCA, self.BCB, bf=True, defer=True)
        if self.kv_pend is not None:
            fn, p = self.kv_pend
            self.kv_pend = None
            fn(*p)
            self.vt_free = (self.sPE, self.sPE.n)
        rb, ab = f1()
        xcd = self.ln_apply_cd(RES, self.BCA, self.BCB, rb, ab,
                               dst_dep_extra=(self.sPE, self.sPE.n))
        xb1_dep = xcd[-1]
        q_last = None
        for fb in range(2):
            off, ld, seg = self.load_wtile("wq", fb)
            for m in range(4):
                c = fb * 4 + m
                bank, d = self.mm_group(off, ld, 8, 512, m, self.XB, xcd[0],
                                        mid_deps={k: xcd[k] for k in range(1, 8)})
                scr0, w0, s0 = self.scr()
                scr1, w1, s1 = self.scr()
                dm = self.op("dve", lambda e, b=bank, o=scr0, c=c: e.tensor_scalar(
                    o[:, :], b[:, :], self.COLS["bqc"][:, c:c + 1], 0.0, ALU.add, ALU.min),
                    deps=[d, w0, self.qb_store], inc=(self.sDVE, 1))
                du = self.op("act", lambda e, b=bank, o=scr1, c=c: e.activation(
                    o[:, :], b[:, :], AF.Identity, bias=self.BQ1[:, c:c + 1]),
                    deps=[d, w1, self.bq1_dep, dm], inc=(self.sACT, 1))
                dx = self.op("act", lambda e, o=scr0: e.activation(o[:, :], o[:, :], AF.Exp),
                             deps=[dm], inc=(self.sACT, 1))
                dq = self.op("dve", lambda e, a=scr0, b2=scr1, c=c: e.tensor_max(
                    self.QB[:, c * GT:(c + 1) * GT], a[:, :], b2[:, :]),
                    deps=[du, dx], inc=(self.sDVE, 1))
                s0(dq)
                s1(dq)
                self.bfree[id(bank)] = dx
                q_last = dq
            seg[2] = (self.sPE, self.sPE.n)
        self.qb_store = self.dma("act", self.qs[:, t0:t0 + GT].rearrange("(c p) t -> p c t", p=128),
                                 self.QB[:, :].rearrange("p (c t) -> p c t", c=NCH), deps=[q_last])
        first = (g == 0)
        last = (g == NGS - 1)
        do_lnpre = i + 1 < len(self.GROUPS) and self.GROUPS[i + 1][0] == "p1"
        pi2 = (i + 1) % 2
        lnv = {}
        vsec = 0

        def lnpre_hook():
            if not do_lnpre:
                return
            if vsec == 2:
                lnv["f"] = self.ln_finish(pi2, self.BCA, self.BCB, bf=True, defer=True)
            elif vsec == 4:
                lnv["rbab"] = lnv["f"]()
            elif vsec in (5, 6, 7, 8):
                rb, ab = lnv["rbab"]
                last = None
                for c in range(2 * (vsec - 5), 2 * (vsec - 5) + 2):
                    last = self.ln_apply_chunk(self.RESP[pi2], self.BCA, self.BCB, rb, ab,
                                               c, self.PO, dst_dep_extra=self.po_free)
                if vsec == 8:
                    self.bfree[id(self.BCA)] = last
                    self.bfree[id(self.BCB)] = last
                    self.pre = {"xb": last}
        for fs in (2, 3):
            off, ld, seg = self.load_wtile("kv", fs)
            for tc in range(TC):
                bank = self.banks.next()
                deps = [self.bfree.get(id(bank)), ld, xb1_dep, self.vt_free]
                for k in range(8):
                    lh = self.XB[:, k * GT + tc * 128: k * GT + (tc + 1) * 128]
                    rh = self.WT[:, off + k * 512:off + (k + 1) * 512]
                    self.op("pe", lambda e, b=bank, l=lh, r=rh, st=(k == 0): e.matmul(
                        b[:, :], l, r, start=st, stop=False),
                        deps=deps if k == 0 else [])
                d = self.op("pe", lambda e, b=bank, fs=fs: e.matmul(
                    b[:, :], self.ONEB[0:1, 0:128], self.BKV[0:1, fs * 512:(fs + 1) * 512],
                    start=False, stop=True), deps=[], inc=(self.sPE, 1))
                dv = self.op("act", lambda e, b=bank, fs=fs, tc=tc: e.activation(
                    self.VT[:, tc * 1024 + (fs - 2) * 512: tc * 1024 + (fs - 1) * 512],
                    b[:, :], AF.Copy), deps=[d], inc=(self.sACT, 1))
                self.bfree[id(bank)] = dv
                vsec += 1
                lnpre_hook()
            seg[2] = (self.sPE, self.sPE.n)
        vt_ready = (self.sACT, self.sACT.n)
        offk = []
        for fs in (0, 1):
            off, ld, seg = self.load_wtile("kv", fs)
            offk.append((off, ld, seg))
        kt_free = [None, None]
        pend = None

        def emit_acc(tc, evac_deps):
            ko = (tc % 2) * 1024
            kdeps = evac_deps + [self.kv_bank_free, vt_ready]
            for pr in range(8):
                kvb = self.KV0 if pr < 4 else self.KV1
                jj = (pr % 4) * 128
                self.op("pe", lambda e, b=kvb, pr=pr, jj=jj, tc=tc, ko=ko,
                        st=(first and tc == 0 and pr % 4 == 0), sp=last: e.matmul(
                    b[:, jj:jj + 128], self.KT[:, ko + pr * 128:ko + (pr + 1) * 128],
                    self.VT[:, tc * 1024 + pr * 128: tc * 1024 + (pr + 1) * 128],
                    start=st, stop=sp, skip_group_check=True),
                    deps=kdeps if pr == 0 else [])
            dd = None
            for pr in range(8):
                dd = self.op("pe", lambda e, pr=pr, ko=ko, st=(first and tc == 0 and pr == 0), sp=last: e.matmul(
                    self.K1P[:, pr:pr + 1], self.KT[:, ko + pr * 128:ko + (pr + 1) * 128],
                    self.ONECB[:, :], start=st, stop=sp, skip_group_check=True),
                    deps=[self.ones_dep, self.kv_bank_free] if pr == 0 else [],
                    inc=(self.sPE, 1) if pr == 7 else None)
            kt_free[tc % 2] = dd

        for tc in range(TC):
            evac_deps = []
            ko = (tc % 2) * 1024
            for fi, (off, ld, seg) in enumerate(offk):
                bank = self.banks.next()
                deps = [self.bfree.get(id(bank)), ld, xb1_dep]
                for k in range(8):
                    lh = self.XB[:, k * GT + tc * 128: k * GT + (tc + 1) * 128]
                    rh = self.WT[:, off + k * 512:off + (k + 1) * 512]
                    self.op("pe", lambda e, b=bank, l=lh, r=rh, st=(k == 0): e.matmul(
                        b[:, :], l, r, start=st, stop=False),
                        deps=deps if k == 0 else [])
                d = self.op("pe", lambda e, b=bank, fs=fi: e.matmul(
                    b[:, :], self.ONEB[0:1, 0:128], self.BKV[0:1, fs * 512:(fs + 1) * 512],
                    start=False, stop=True), deps=[], inc=(self.sPE, 1))
                scr0, w0, s0 = self.scr()
                scr1, w1, s1 = self.scr()
                dm = self.op("dve", lambda e, b=bank, o=scr0: e.tensor_scalar(
                    o[:, :], b[:, :], 0.0, None, ALU.min), deps=[d, w0], inc=(self.sDVE, 1))
                du = self.op("act", lambda e, b=bank, o=scr1: e.activation(
                    o[:, :], b[:, :], AF.Copy, bias=1.0), deps=[d, w1, dm], inc=(self.sACT, 1))
                dx = self.op("act", lambda e, o=scr0: e.activation(o[:, :], o[:, :], AF.Exp),
                             deps=[dm], inc=(self.sACT, 1))
                dk = self.op("dve", lambda e, a=scr0, b2=scr1, fs=fi, ko=ko: e.tensor_max(
                    self.KT[:, ko + fs * 512:ko + (fs + 1) * 512], a[:, :], b2[:, :]),
                    deps=[du, dx, kt_free[tc % 2]], inc=(self.sDVE, 1))
                s0(dk)
                s1(dk)
                self.bfree[id(bank)] = dx
                evac_deps.append(dk)
            if pend is not None:
                emit_acc(*pend)
            pend = (tc, evac_deps)
        if last:
            emit_acc(*pend)
        else:
            self.kv_pend = (emit_acc, pend)
        for off, ld, seg in offk:
            seg[2] = (self.sPE, self.sPE.n)
        self.vt_free = (self.sPE, self.sPE.n)
        self.res_free[pi] = [(self.sDVE, self.sDVE.n), (self.sACT, self.sACT.n)]

    def kv_evac(self, s):
        kv_done = (self.sPE, self.sPE.n)
        self.op("dve", lambda e, s=s: e.memset(self.K1M2[:, s * 264:(s + 1) * 264], 0.0),
                deps=[kv_done])
        for pr in range(8):
            self.op("dve", lambda e, s=s, pr=pr: e.tensor_copy(
                self.KVE[0:64, s * 512 + pr * 64: s * 512 + (pr + 1) * 64],
                (self.KV0 if pr < 4 else self.KV1)[0:64, (pr % 4) * 128:(pr % 4) * 128 + 64]),
                deps=[])
            self.op("dve", lambda e, s=s, pr=pr: e.tensor_copy(
                self.KVE[64:128, s * 512 + pr * 64: s * 512 + (pr + 1) * 64],
                (self.KV0 if pr < 4 else self.KV1)[64:128, (pr % 4) * 128 + 64:(pr % 4) * 128 + 128]),
                deps=[])
            base = s * 264 + pr * 33
            self.op("dve", lambda e, base=base, pr=pr: e.tensor_copy(
                self.K1M2[0:64, base: base + 1], self.K1P[0:64, pr:pr + 1]), deps=[])
            inc = (self.sDVE, 1) if pr == 7 else None
            self.kv_bank_free = self.op("dve", lambda e, base=base, pr=pr: e.tensor_copy(
                self.K1M2[64:128, base + 32: base + 33], self.K1P[64:128, pr:pr + 1]),
                deps=[], inc=inc)
        self.kve_dep = self.kv_bank_free

    def p2_island_setup(self, j, qslab):
        kind, s, g = self.GROUPS[j]
        QP = self.PO
        qperm = self.op("dve", lambda e: e.tensor_copy(
            QP[:, :].rearrange("p (j m) -> p j m", j=16),
            self.QB[:, :].rearrange("p (m j) -> p j m", j=16)),
            deps=[qslab, self.qp2_free, self.po_free], inc=(self.sDVE, 1))
        self.qperm_dep = qperm
        for fb in range(2):
            ld = self.dma("sp", self.WPT[:, fb * 4096:(fb + 1) * 4096],
                          self.wb16["wp"][fb], deps=[self.wp_free, self.conv[("wp", fb)]])
            self.wp_ld[fb] = ld
        self.isl = {"j": j, "s": s, "g": g, "QP": QP, "qperm": qperm, "xb_dep": None}

    def p2_island_cc_a(self, cc):
        isl = self.isl
        s, g, QP, qperm = isl["s"], isl["g"], isl["QP"], isl["qperm"]
        qk_deps = []
        for pj in range(2):
            j0 = 2 * cc + pj
            dqk = self.op("pe", lambda e, s=s, g=g, pj=pj, j0=j0, QP=QP: e.matmul(
                self.K1P[0:33, pj * 256:(pj + 1) * 256],
                self.K1M2[:, s * 264 + g * 33: s * 264 + (g + 1) * 33],
                QP[:, j0 * 256:(j0 + 1) * 256], start=True, stop=True, skip_group_check=True),
                deps=[qperm, self.kve_dep, self.kv_bank_free, self.k1p_free], inc=(self.sPE, 1))
            dmx = self.op("dve", lambda e, pj=pj: e.tensor_scalar(
                self.REC33[:, pj * 256:(pj + 1) * 256], self.K1P[0:33, pj * 256:(pj + 1) * 256],
                1e-6, None, ALU.max), deps=[dqk])
            def _recip(e, pj=pj):
                with e.bass.allow_low_precision(reason="bf16 reciprocal feeds bf16 attn matmuls"):
                    return e.reciprocal(self.REC33[:, pj * 256:(pj + 1) * 256],
                                        self.REC33[:, pj * 256:(pj + 1) * 256])
            drc = self.op("dve", _recip, deps=[], inc=(self.sDVE, 1))
            qk_deps.append(drc)
        self.k1p_free = qk_deps[-1]
        isl["qk_deps"] = qk_deps

    def p2_island_cc_b(self, cc):
        isl = self.isl
        s, g, QP = isl["s"], isl["g"], isl["QP"]
        qk_deps = isl["qk_deps"]
        rb_b = self.banks.next()
        nb = self.banks.next()
        last_mm = None
        for pj in range(2):
            for th in range(2):
                row = 0 if th == 0 else 32
                self.op("pe", lambda e, b=rb_b, pj=pj, th=th, row=row: e.matmul(
                    b[pj * 64:(pj + 1) * 64, th * 256:(th + 1) * 256],
                    self.ONE33[row:row + 1, 0:64],
                    self.REC33[row:row + 1, pj * 256:(pj + 1) * 256],
                    start=True, stop=True, skip_group_check=True),
                    deps=[qk_deps[pj], self.bfree.get(id(rb_b))])
                last_mm = self.op("pe", lambda e, b=nb, s=s, g=g, pj=pj, th=th, j0=2 * cc + pj, QP=QP: e.matmul(
                    b[pj * 64:(pj + 1) * 64, th * 256:(th + 1) * 256],
                    self.KVE[th * 64:(th + 1) * 64, s * 512 + g * 64:s * 512 + (g + 1) * 64],
                    QP[th * 64:(th + 1) * 64, j0 * 256:(j0 + 1) * 256],
                    start=True, stop=True, skip_group_check=True),
                    deps=[self.bfree.get(id(nb))],
                    inc=(self.sPE, 1) if (pj == 1 and th == 1) else None)
        dall = last_mm
        dc = self.op("act", lambda e, b=rb_b: e.activation(self.REC[:, :], b[:, :], AF.Copy),
                     deps=[dall, self.rec_free], inc=(self.sACT, 1))
        self.bfree[id(rb_b)] = dc
        dx = self.op("dve", lambda e, b=nb, cc=cc: e.tensor_mul(
            self.XB[:, cc * GT:(cc + 1) * GT], b[:, :], self.REC[:, :]),
            deps=[dall, dc], inc=(self.sDVE, 1))
        self.bfree[id(nb)] = dx
        self.rec_free = dx
        isl["xb_dep"] = dx

    def p2_island_tail_proj(self):
        isl = self.isl
        pi = isl["j"] % 2
        xb_attn_dep = isl["xb_dep"]
        self.qp2_free = (self.sPE, self.sPE.n)
        for fb in range(2):
            off, ld = fb * 4096, self.wp_ld[fb]
            for m in range(4):
                c = fb * 4 + m
                bank, d = self.mm_group(off, ld, 8, 512, m, self.XB, xb_attn_dep,
                                        arena=self.WPT)
                dp = self.op("act", lambda e, b=bank, c=c: e.activation(
                    self.PO[:, c * GT:(c + 1) * GT], b[:, :], AF.Identity,
                    bias=self.COLS["bpc"][:, c:c + 1]), deps=[d, self.po_free], inc=(self.sACT, 1))
                self.bfree[id(bank)] = dp
                self.stats_chunk(pi, self.PO, c, c == 0, dp)
        self.wp_free = (self.sPE, self.sPE.n)

    def p2_island_tail_x2_chunk(self, c, rb2, ab2):
        isl = self.isl
        j = isl["j"]
        pi = j % 2
        RES = self.RESP[pi]
        rb1 = self.pre["rb1"]
        ab1 = self.pre["ab1"]
        scr0, w0, s0 = self.scr()
        scr1, w1, s1 = self.scr()
        scr2, w2, s2 = self.scr()
        self.op("dve", lambda e, c=c, o=scr0: e.tensor_mul(
            o[:, :], self.PO[:, c * GT:(c + 1) * GT], self.BCA[:, :]), deps=[rb2, w0])
        dz0 = self.op("dve", lambda e, o=scr0: e.tensor_add(o[:, :], o[:, :], self.BCB[:, :]),
                      deps=[ab2], inc=(self.sDVE, 1))
        dz = self.op("act", lambda e, o=scr0, z=scr1, c=c: e.activation(
            z[:, :], o[:, :], AF.Identity, scale=self.COLS["gamc"][:, c:c + 1],
            bias=self.COLS["betc"][:, c:c + 1]), deps=[dz0, w1], inc=(self.sACT, 1))
        s0(dz)
        self.op("dve", lambda e, o=scr2, c=c, RES=RES: e.tensor_mul(
            o[:, :], RES[:, c * GT:(c + 1) * GT], self.KV0[:, :]), deps=[rb1, w2])
        dyv = self.op("dve", lambda e, o=scr2: e.tensor_add(o[:, :], o[:, :], self.KV1[:, :]),
                      deps=[ab1], inc=(self.sDVE, 1))
        self.op("dve", lambda e, z=scr1, o=scr2: e.tensor_add(z[:, :], z[:, :], o[:, :]),
                deps=[dz, dyv])
        dx2 = self.op("dve", lambda e, z=scr1, c=c, RES=RES: e.tensor_add(
            RES[:, c * GT:(c + 1) * GT], z[:, :], RES[:, c * GT:(c + 1) * GT]),
            deps=[], inc=(self.sDVE, 1))
        s1(dx2)
        s2(dx2)
        self.stats_chunk(pi, RES, c, c == 0, dx2)
        if c == NCH - 1:
            x2_done = (self.sDVE, self.sDVE.n)
            self.bfree[id(self.BCA)] = x2_done
            self.bfree[id(self.BCB)] = x2_done
            self.bfree[id(self.KV0)] = x2_done
            self.bfree[id(self.KV1)] = x2_done
            self.po_free = x2_done
            self.island_j = j

    def p2_island_tail_x2(self, rb2, ab2):
        for c in range(NCH):
            self.p2_island_tail_x2_chunk(c, rb2, ab2)

    def p2_island(self, j):
        self.p2_island_setup(j, self.pre["qslab"])
        for cc in range(NCH):
            self.p2_island_cc_a(cc)
            self.p2_island_cc_b(cc)
        self.p2_island_tail_proj()
        rb2, ab2 = self.ln_finish(j % 2, self.BCA, self.BCB)
        self.p2_island_tail_x2(rb2, ab2)

    def p2_island_b_part1(self):
        return self.ln_finish(self.island_j % 2, self.BCA, self.BCB, bf=True, defer=True)

    def p2_island_b_part2(self, fn):
        j = self.island_j
        pi = j % 2
        RES = self.RESP[pi]
        rb3, ab3 = fn()
        self.pre2 = self.ln_apply(RES, self.BCA, self.BCB, rb3, ab3,
                                  dst_dep_extra=(self.sPE, self.sPE.n))

    def p2_island_b(self):
        j = self.island_j
        pi = j % 2
        RES = self.RESP[pi]
        rb3, ab3 = self.ln_finish(pi, self.BCA, self.BCB, bf=True)
        self.pre2 = self.ln_apply(RES, self.BCA, self.BCB, rb3, ab3,
                                  dst_dep_extra=(self.sPE, self.sPE.n))

    def p2_main(self, i, s, g):
        t0 = s * SEQ + g * GT
        pi = i % 2
        RES = self.RESP[pi]
        xb2_dep = self.pre2
        self.banks.set_width(5)
        hin_dep = self.stage_w1w2(self.XB, xb2_dep)
        self.emit_prefetch(i + 1)
        hooks = None
        if i + 1 < len(self.GROUPS):
            self.p2_island_setup(i + 1, self.pre_qslab)

            def hookfn(k):
                def h():
                    if k > 0:
                        self.p2_island_cc_b(k - 1)
                    if k < NCH:
                        self.p2_island_cc_a(k)
                return h
            hooks = {12 + k: hookfn(k) for k in range(NCH + 1)}
            pi2 = (i + 1) % 2
            lnp = {}

            def lnpre1():
                lnp["f"] = self.ln_finish(pi2, self.KV0, self.KV1, defer=True)

            def lnpre2():
                rb1, ab1 = lnp["f"]()
                self.pre = {"rb1": rb1, "ab1": ab1}
            hooks[22] = lnpre1
            hooks[24] = lnpre2
        hout_dep, hin_read = self.stage_h("h1", self.HIN, hin_dep, self.HOUT, self.hout_free, "sh1c", hooks=hooks)
        self.banks.set_width(3)
        h2_hooks = None
        if i + 1 < len(self.GROUPS):
            self.p2_island_tail_proj()
            lnq = {}
            h2_hooks = {
                0: lambda: lnq.update(f=self.ln_finish((i + 1) % 2, self.BCA, self.BCB, defer=True)),
                1: lambda: lnq.update(rbab=lnq["f"]()),
            }

            def x2h(c):
                return lambda: self.p2_island_tail_x2_chunk(c, *lnq["rbab"])
            for c in range(NCH):
                h2_hooks[2 + c] = x2h(c)
            h2_hooks[10] = lambda: self.banks.set_width(5)
        hin2_dep, hout_read = self.stage_h("h2", self.HOUT, hout_dep, self.HIN, hin_read, "sh2c", hooks=h2_hooks)
        self.hout_free = hout_read
        self.banks.set_width(3)
        w3_hooks = None
        if i + 1 < len(self.GROUPS):
            ib = {}
            w3_hooks = {1: lambda: ib.update(f=self.p2_island_b_part1()),
                        2: lambda: self.p2_island_b_part2(ib["f"])}
        x3_dep = self.stage_w3(RES, pi, self.HIN, hin2_dep, with_stats=False, hooks=w3_hooks)
        self.hin_free = (self.sPE, self.sPE.n)
        self.dma("act", self.y[:, t0:t0 + GT].rearrange("(c p) t -> p c t", p=128),
                 RES[:, :].rearrange("p (c t) -> p c t", c=NCH), deps=[x3_dep])
        self.res_free[pi] = [x3_dep]


def _tile_w(w, KC, FBW):
    out_f, in_f = w.shape
    NFB = out_f // FBW
    wt = np.ascontiguousarray(w.T)
    arr = wt.reshape(KC, 128, NFB, FBW).transpose(2, 1, 0, 3)
    return np.ascontiguousarray(arr.reshape(NFB, 128, KC * FBW))


def _col(s, n):
    return np.ascontiguousarray(np.asarray(s, np.float32).reshape(n, 128).T)


def kernel(L0_attn_H, L0_attn_W_qkv, L0_attn_b_qkv, L0_attn_W_proj, L0_attn_b_proj,
           L0_attn_gamma, L0_attn_beta, L0_ffn_w1_w, L0_ffn_w1_s, L0_ffn_w2_w,
           L0_ffn_w2_s, L0_ffn_h1_w, L0_ffn_h1_s, L0_ffn_h2_w, L0_ffn_h2_s,
           L0_ffn_w3_w, L0_ffn_w3_s):
    f = lambda a: np.asarray(a, np.float32)
    H = f(L0_attn_H)
    Wqkv = f(L0_attn_W_qkv)

    weights = {
        "w1t": _tile_w(f(L0_ffn_w1_w), *WCFG["w1"][:2]),
        "w2t": _tile_w(f(L0_ffn_w2_w), *WCFG["w2"][:2]),
        "h1t": _tile_w(f(L0_ffn_h1_w), *WCFG["h1"][:2]),
        "h2t": _tile_w(f(L0_ffn_h2_w), *WCFG["h2"][:2]),
        "w3t": _tile_w(f(L0_ffn_w3_w), *WCFG["w3"][:2]),
        "kvt": _tile_w(Wqkv[1024:3072], *WCFG["kv"][:2]),
        "wqt": _tile_w(Wqkv[0:1024], *WCFG["wq"][:2]),
        "wpt": _tile_w(f(L0_attn_W_proj), *WCFG["wp"][:2]),
        "bkv": np.ascontiguousarray(f(L0_attn_b_qkv)[1024:3072].reshape(1, 2048)),
        "s1c": _col(L0_ffn_w1_s, 32), "s2c": _col(L0_ffn_w2_s, 32),
        "sh1c": _col(L0_ffn_h1_s, 32), "sh2c": _col(L0_ffn_h2_s, 32),
        "s3c": _col(L0_ffn_w3_s, 8),
        "gamc": _col(L0_attn_gamma, 8), "betc": _col(L0_attn_beta, 8),
        "bqc": _col(f(L0_attn_b_qkv)[0:1024], 8),
        "bpc": _col(L0_attn_b_proj, 8),
    }
    in_maps = []
    for i in range(NCORES):
        xs = np.ascontiguousarray(H[SPC * i:SPC * (i + 1)].reshape(SPC * SEQ, D).T)
        m = {"x0": xs}
        m.update(weights)
        in_maps.append(m)

    mk = MK()
    nc = mk.build()
    res = run_bass_kernel_spmd(nc, in_maps, core_ids=list(range(NCORES)))
    globals()["_LAST"] = res
    outs = []
    for i in range(NCORES):
        yf = res.results[i]["y"]
        outs.append(np.ascontiguousarray(yf.T).reshape(SPC, SEQ, D))
    return np.concatenate(outs, axis=0)


if __name__ == "__main__":
    mk = MK()
    nc = mk.build()
    print("built n_inst =", nc.n_instructions())


# version 38
# speedup vs baseline: 1.1776x; 1.1776x over previous
import numpy as np
from contextlib import ExitStack
import concourse.bass as bass
import concourse.mybir as mybir
from concourse.bass_utils import run_bass_kernel_spmd

F32 = mybir.dt.float32
BF16 = mybir.dt.bfloat16
AF = mybir.ActivationFunctionType
ALU = mybir.AluOpType

NCORES = 8
D, DH = 1024, 4096
NCH, HCH = 8, 32
SEQ = 4096
SPC = 2
GT = 512
TC = GT // 128
NGS = SEQ // GT
EPS = 1e-5
WTELEMS = 12288

WCFG = {
    "w1": (8, 256, 16), "w2": (8, 256, 16),
    "h1": (32, 128, 32), "h2": (32, 128, 32),
    "w3": (32, 128, 8),
    "kv": (8, 512, 4), "wq": (8, 512, 2), "wp": (8, 512, 2),
}
WQUEUE = {"w1": "sp", "w2": "sp", "h1": "sp", "kv": "sp", "wq": "sp", "wp": "sp",
          "h2": "pool", "w3": "pool"}

DEBUG_X1 = False


class Sem:
    def __init__(self, h):
        self.h = h
        self.n = 0

    def bump(self, k):
        self.n += k
        return self.n


class Banks:
    def __init__(self, tiles, store):
        self.tiles = tiles
        self.store = store
        self.i = 0
        self.width = 3

    def next(self):
        t = self.tiles[self.i % self.width]
        self.i += 1
        return t

    def set_width(self, w):
        self.width = w
        self.i = 0


class WTRing:
    def __init__(self, total):
        self.total = total
        self.pos = 0
        self.segs = []

    def alloc(self, n):
        if self.pos + n > self.total:
            self.pos = 0
        off = self.pos
        self.pos += n
        deps = [s[2] for s in self.segs if s[0] < off + n and s[1] > off and s[2] is not None]
        self.segs = [s for s in self.segs if not (s[0] >= off and s[1] <= off + n)]
        seg = [off, off + n, None]
        self.segs.append(seg)
        return off, deps, seg


class MK:
    def __init__(self):
        nc = bass.Bass("TRN2", target_bir_lowering=False)
        self.nc = nc
        self.ops = {e: [] for e in ("pe", "dve", "act", "sp", "pool")}
        self.waited = {}

        dt = nc.dram_tensor
        self.x0 = dt("x0", [D, SPC * SEQ], F32, kind="ExternalInput").ap()
        self.y = dt("y", [D, SPC * SEQ], F32, kind="ExternalOutput").ap()
        x1kind = "ExternalOutput" if DEBUG_X1 else "Internal"
        self.x1s = dt("x1s", [D, SPC * SEQ], F32, kind=x1kind).ap()
        self.qs = dt("qs", [D, SPC * SEQ], BF16, kind="Internal").ap()
        self.wf32 = {}
        self.wb16 = {}
        for name, (KC, FBW, NFB) in WCFG.items():
            self.wf32[name] = dt(name + "t", [NFB, 128, KC * FBW], F32, kind="ExternalInput").ap()
            self.wb16[name] = dt(name + "b", [NFB, 128, KC * FBW], BF16, kind="Internal").ap()
        self.bkv_f = dt("bkv", [1, 2048], F32, kind="ExternalInput").ap()
        self.bkv_b = dt("bkvb", [1, 2048], BF16, kind="Internal").ap()
        self.cols_in = {}
        for cname, w in [("s1c", 32), ("s2c", 32), ("sh1c", 32), ("sh2c", 32),
                         ("s3c", NCH), ("gamc", NCH), ("betc", NCH),
                         ("bqc", NCH), ("bpc", NCH)]:
            self.cols_in[cname] = dt(cname, [128, w], F32, kind="ExternalInput").ap()

        self.es = ExitStack()

    def build(self):
        nc = self.nc
        es = self.es
        sb = lambda name, shape, dtyp: es.enter_context(nc.sbuf_tensor(name, shape, dtyp))
        self.RESP = [sb("RES0", [128, NCH * GT], F32), sb("RES1", [128, NCH * GT], F32)]
        self.XB = sb("XB", [128, NCH * GT], BF16)
        self.QB = sb("QB", [128, NCH * GT], BF16)
        self.PO = sb("PO", [128, NCH * GT], BF16)
        self.HIN = sb("HIN", [128, HCH * GT], BF16)
        self.HOUT = sb("HOUT", [128, HCH * GT], BF16)
        self.WT = sb("WT", [128, WTELEMS], BF16)
        self.WPT = sb("WPT", [128, 8192], BF16)
        self.KT = sb("KT", [128, 2048], BF16)
        self.VT = sb("VT", [128, 4096], BF16)
        self.KVE = sb("KVE", [128, 2 * 512], BF16)
        self.K1M2 = sb("K1M2", [128, 2 * 8 * 33], BF16)
        self.REC33 = sb("REC33", [33, GT], BF16)
        self.REC = sb("REC", [128, GT], BF16)
        self.SCR = [sb(f"SCR{i}", [128, GT], F32) for i in range(4)]
        self.scr_free = [None] * 4
        self.scr_i = 0
        self.SUMP = [(sb("SUMA", [128, GT], F32), sb("SQA", [128, GT], F32)),
                     (sb("SUMB", [128, GT], F32), sb("SQB", [128, GT], F32))]
        self.ROWS = sb("ROWS", [33, 3 * GT], F32)
        self.ROWSB = sb("ROWSB", [33, 2 * GT], BF16)
        self.BKV = sb("BKV", [1, 2048], BF16)
        self.ONER = sb("ONER", [33, 128], F32)
        self.ONE33 = sb("ONE33", [33, 64], BF16)
        self.ONEB = sb("ONEB", [33, 128], BF16)
        self.ONEC = sb("ONEC", [128, 1], F32)
        self.ONECB = sb("ONECB", [128, 1], BF16)
        self.EPSC = sb("EPSC", [128, 1], F32)
        self.COLS = {}
        for cname in self.cols_in:
            w = 32 if cname.startswith("s") and cname not in ("s3c",) else NCH
            self.COLS[cname] = sb("C_" + cname, [128, w], F32)
        self.S3H = sb("S3H", [128, NCH], F32)
        self.BQ1 = sb("BQ1", [128, NCH], F32)

        ps = lambda name: es.enter_context(nc.psum_tensor(name, [128, 512], F32))
        mA, mB, mC = ps("mA"), ps("mB"), ps("mC")
        self.KV0 = ps("kv0")
        self.KV1 = ps("kv1")
        self.K1P = ps("k1p")
        self.BCA = ps("bcA")
        self.BCB = ps("bcB")
        self.bfree = {}
        self.banks = Banks([mA, mB, mC, self.BCA, self.BCB], self.bfree)

        mksem = lambda n: Sem(es.enter_context(nc.semaphore(n)))
        self.sPE = mksem("sPE")
        self.sDVE = mksem("sDVE")
        self.sACT = mksem("sACT")
        self.qsems = {q: [mksem(f"{q}{i}") for i in range(3)] for q in ("sp", "pool", "act")}
        self.qcnt = {q: 0 for q in ("sp", "pool", "act")}
        self.own = {"pe": self.sPE, "dve": self.sDVE, "act": self.sACT}
        self.esem = {"pe": self.sPE, "dve": self.sDVE, "act": self.sACT}

        self.ring = WTRing(WTELEMS)

        self.hin_free = None
        self.hout_free = None
        self.kv_bank_free = None
        self.rows_free = None
        self.qb_store = None
        self.k1p_free = None
        self.rec_free = None
        self.po_free = None

        self.res_free = [[], []]
        self.kv_pend = None
        self.wp_free = None
        self.wp_ld = [None, None]
        self.qp2_free = None
        self.pre2 = None
        self.vt_free = None
        self.sum_free = [None, None]
        self.stats_dep_p = [None, None]

        self.GROUPS = ([("p1", s, g) for s in range(SPC) for g in range(NGS)]
                       + [("p2", s, g) for s in range(SPC) for g in range(NGS)])
        self.pre = {}
        self.qperm_dep = None
        self.emit_prefetch(0)
        self.preamble()
        self.emit_lnpre(0)
        for i, (kind, s, g) in enumerate(self.GROUPS):
            if kind == "p1":
                self.p1_main(i, s, g)
                if g == NGS - 1:
                    self.kv_evac(s)
                    if s == SPC - 1:
                        self.bfree[id(self.KV0)] = self.kv_bank_free
                        self.bfree[id(self.KV1)] = self.kv_bank_free
                        self.emit_lnpre(i + 1)
                        self.p2_island(i + 1)
                        self.p2_island_b()
            else:
                self.p2_main(i, s, g)
        for s_ in self.qsems["act"]:
            self.emit("act", lambda e, h=s_.h, v=s_.n: e.wait_ge(h, v))

        with nc.Block() as blk:
            blk.tensor(lambda e: [f(e) for f in self.ops["pe"]] and None)
            blk.vector(lambda e: [f(e) for f in self.ops["dve"]] and None)
            blk.scalar(lambda e: [f(e) for f in self.ops["act"]] and None)
            blk.sync(lambda e: [f(e) for f in self.ops["sp"]] and None)
            blk.gpsimd(lambda e: [f(e) for f in self.ops["pool"]] and None)
        self.es.close()
        return nc

    def emit(self, eng, fn):
        self.ops[eng].append(fn)

    def wait(self, eng, dep):
        if dep is None:
            return
        sem, val = dep
        if val <= 0 or sem is self.own.get(eng):
            return
        key = (eng, id(sem))
        if self.waited.get(key, 0) >= val:
            return
        self.waited[key] = val
        self.emit(eng, lambda e, h=sem.h, v=val: e.wait_ge(h, v))

    def op(self, eng, build, deps=(), inc=None):
        for d in deps:
            self.wait(eng, d)
        if inc is not None:
            sem, k = inc
            val = sem.bump(k)

            def run(e, b=build, h=sem.h, kk=k):
                b(e).then_inc(h, kk)
            self.emit(eng, run)
            return (sem, val)
        else:
            self.emit(eng, lambda e, b=build: b(e))
            return None

    def dma(self, q, dst, src, deps=()):
        sem = self.qsems[q][self.qcnt[q] % 3]
        self.qcnt[q] += 1
        self.wait(q, (sem, sem.n))
        return self.op(q, lambda e, d=dst, s=src: e.dma_start(d, s), deps, inc=(sem, 16))

    def barrier(self):
        snap = {e: s.n for e, s in self.esem.items()}
        for e in self.esem:
            for o, v in snap.items():
                if o != e:
                    self.wait(e, (self.esem[o], v))

    def scr(self):
        i = self.scr_i % 4
        self.scr_i += 1
        tile = self.SCR[i]
        dep = self.scr_free[i]

        def setter(d, i=i):
            self.scr_free[i] = d
        return tile, dep, setter

    def preamble(self):
        nc = self.nc
        for cname, ap in self.cols_in.items():
            self.dma("act", self.COLS[cname][:, :], ap)
        crit = [(s_, s_.n) for s_ in self.qsems["act"]]
        self.conv = {}
        for d_ in crit:
            self.wait("pool", d_)
        self.conv_bkv = self.dma("pool", self.bkv_b[:, :], self.bkv_f[:, :])
        pieces = [("w1", 0, 1), ("w2", 0, 1), ("w1", 1, 2), ("w2", 1, 2),
                  ("w1", 2, 3), ("w2", 2, 3), ("w1", 3, 4), ("w2", 3, 4),
                  ("h1", 0, 2), ("h2", 0, 2), ("w1", 4, 7), ("w2", 4, 7),
                  ("h1", 2, 4), ("h2", 2, 4), ("w1", 7, 10), ("w2", 7, 10),
                  ("h1", 4, 6), ("h2", 4, 6), ("w1", 10, 13), ("w2", 10, 13),
                  ("h1", 6, 8), ("h2", 6, 8), ("w1", 13, 16), ("w2", 13, 16),
                  ("h1", 8, 10), ("h2", 8, 10), ("w3", 0, 4),
                  ("h1", 10, 14), ("h2", 10, 14), ("w3", 4, 8),
                  ("kv", 0, 4), ("wq", 0, 2), ("wp", 0, 2),
                  ("h1", 14, 22), ("h2", 14, 22), ("h1", 22, 32), ("h2", 22, 32)]
        for name, f0, f1 in pieces:
            d = self.dma("pool", self.wb16[name][f0:f1], self.wf32[name][f0:f1])
            for fb in range(f0, f1):
                self.conv[(name, fb)] = d
        dl = self.dma("act", self.BKV[:, :], self.bkv_b[:, :], deps=[self.conv_bkv])
        cdeps = [(s_, s_.n) for s_ in self.qsems["act"]]
        for cdep in cdeps:
            self.wait("pe", cdep)
            self.wait("dve", cdep)
        self.op("act", lambda e: e.activation(self.S3H[:, :], self.COLS["s3c"][:, :], AF.Copy, scale=0.5),
                deps=cdeps, inc=(self.sACT, 1))
        self.bq1_dep = self.op("act", lambda e: e.activation(self.BQ1[:, :], self.COLS["bqc"][:, :], AF.Copy, bias=1.0),
                               deps=cdeps, inc=(self.sACT, 1))
        self.op("dve", lambda e: e.memset(self.ONER[:, :], 1.0))
        self.op("dve", lambda e: e.memset(self.ONE33[:, :], 1.0))
        self.op("dve", lambda e: e.memset(self.ONEB[:, :], 1.0))
        self.op("dve", lambda e: e.memset(self.ONEC[:, :], 1.0))
        self.op("dve", lambda e: e.memset(self.EPSC[:, :], EPS))
        self.ones_dep = self.op("dve", lambda e: e.memset(self.ONECB[:, :], 1.0), inc=(self.sDVE, 1))
        self.barrier()

    def stats_chunk(self, pi, src, c, first, src_dep):
        SUMA, SQA = self.SUMP[pi]
        s = src[:, c * GT:(c + 1) * GT]
        scr, wdep, sset = self.scr()
        sq = self.op("act", lambda e, s=s, o=scr: e.activation(o[:, :], s, AF.Square),
                     deps=[src_dep, wdep], inc=(self.sACT, 1))
        if first:
            d1 = self.op("dve", lambda e, s=s, SUMA=SUMA: e.tensor_copy(SUMA[:, :], s),
                         deps=[src_dep, self.sum_free[pi]], inc=(self.sDVE, 1))
            d2 = self.op("dve", lambda e, o=scr, SQA=SQA: e.tensor_copy(SQA[:, :], o[:, :]),
                         deps=[sq], inc=(self.sDVE, 1))
        else:
            d1 = self.op("dve", lambda e, s=s, SUMA=SUMA: e.tensor_add(SUMA[:, :], SUMA[:, :], s),
                         deps=[src_dep], inc=(self.sDVE, 1))
            d2 = self.op("dve", lambda e, o=scr, SQA=SQA: e.tensor_add(SQA[:, :], SQA[:, :], o[:, :]),
                         deps=[sq], inc=(self.sDVE, 1))
        sset(d2)
        self.stats_dep_p[pi] = (self.sDVE, self.sDVE.n)

    def ln_finish(self, pi, rb_bank, ab_bank, bf=False, defer=False):
        SUMA, SQA = self.SUMP[pi]
        R = self.ROWS
        r_mean = R[32:33, 0:GT]
        r_sc = r_var = R[32:33, GT:2 * GT]
        bfree = self.bfree.get(id(rb_bank))
        bfree2 = self.bfree.get(id(ab_bank))
        d_sum = self.op("pe", lambda e, b=rb_bank, SUMA=SUMA: e.matmul(b[0:1, :], self.ONEC[:, :], SUMA[:, :], start=True, stop=True),
                        deps=[self.stats_dep_p[pi], bfree, self.ones_dep], inc=(self.sPE, 1))
        d_sq = self.op("pe", lambda e, b=rb_bank, SQA=SQA: e.matmul(b[32:33, :], self.ONEC[:, :], SQA[:, :], start=True, stop=True),
                       deps=[], inc=(self.sPE, 1))
        if bf:
            out_r = self.ROWSB[32:33, 0:GT]
            out_a = self.ROWSB[32:33, GT:2 * GT]
            on_r = on_a = self.ONEB[32:33, 0:128]
        else:
            out_r = r_var
            out_a = R[32:33, 2 * GT:3 * GT]
            on_r = on_a = self.ONER[32:33, 0:128]
        self.op("dve", lambda e, b=rb_bank, o=r_mean: e.tensor_scalar(o, b[0:1, :], 1.0 / D, None, ALU.mult),
                deps=[d_sq, self.rows_free])
        self.op("dve", lambda e, o=r_sc, i=r_mean: e.tensor_tensor(o, i, i, ALU.mult), deps=[])
        d1 = self.op("dve", lambda e, b=rb_bank, o=r_var, i=r_sc: e.scalar_tensor_tensor(
            o, b[32:33, :], 1.0 / D, i, ALU.mult, ALU.subtract), deps=[], inc=(self.sDVE, 1))
        self.op("act", lambda e, o=r_var, i=r_var: e.activation(o, i, AF.Ln, bias=self.EPSC[32:33, 0:1]),
                deps=[d1])
        dsq = self.op("act", lambda e, o=out_r, i=r_var: e.activation(o, i, AF.Exp, scale=-0.5),
                      deps=[], inc=(self.sACT, 1))
        drow = self.op("dve", lambda e, o=out_a, m=r_mean, r=out_r: e.scalar_tensor_tensor(o, m, -1.0, r, ALU.mult, ALU.mult),
                       deps=[dsq], inc=(self.sDVE, 1))

        def part2():
            rb_dep = self.op("pe", lambda e, b=rb_bank, r=out_r, o=on_r: e.matmul(b[:, :], o, r, start=True, stop=True),
                             deps=[dsq, d1], inc=(self.sPE, 1))
            ab_dep = self.op("pe", lambda e, b=ab_bank, a=out_a, o=on_a: e.matmul(b[:, :], o, a, start=True, stop=True),
                             deps=[drow, bfree2], inc=(self.sPE, 1))
            self.rows_free = ab_dep
            return rb_dep, ab_dep
        self.sum_free[pi] = d_sq
        if defer:
            return part2
        return part2()

    def ln_apply(self, src, rb_bank, ab_bank, rb_dep, ab_dep, dst_dep_extra=None, dst=None):
        last = self.ln_apply_cd(src, rb_bank, ab_bank, rb_dep, ab_dep, dst_dep_extra, dst)[-1]
        return last

    def ln_apply_chunk(self, src, rb_bank, ab_bank, rb_dep, ab_dep, c, dst, dst_dep_extra=None):
        s = src[:, c * GT:(c + 1) * GT]
        scr, wdep, sset = self.scr()
        self.op("dve", lambda e, s=s, o=scr, b=rb_bank: e.tensor_mul(o[:, :], s, b[:, :]),
                deps=[rb_dep, wdep, dst_dep_extra])
        last = self.op("dve", lambda e, o=scr, b=ab_bank, c=c, dst=dst: e.tensor_add(
            dst[:, c * GT:(c + 1) * GT], o[:, :], b[:, :]), deps=[ab_dep], inc=(self.sDVE, 1))
        sset(last)
        return last

    def ln_apply_cd(self, src, rb_bank, ab_bank, rb_dep, ab_dep, dst_dep_extra=None, dst=None):
        if dst is None:
            dst = self.XB
        cds = []
        for c in range(NCH):
            s = src[:, c * GT:(c + 1) * GT]
            scr, wdep, sset = self.scr()
            self.op("dve", lambda e, s=s, o=scr, b=rb_bank: e.tensor_mul(o[:, :], s, b[:, :]),
                    deps=[rb_dep, wdep, dst_dep_extra])
            last = self.op("dve", lambda e, o=scr, b=ab_bank, c=c, dst=dst: e.tensor_add(
                dst[:, c * GT:(c + 1) * GT], o[:, :], b[:, :]), deps=[ab_dep], inc=(self.sDVE, 1))
            sset(last)
            cds.append(last)
        self.bfree[id(rb_bank)] = cds[-1]
        self.bfree[id(ab_bank)] = cds[-1]
        return cds

    def load_wtile(self, name, fb):
        KC, FBW, NFB = WCFG[name]
        n = KC * FBW
        off, wdeps, seg = self.ring.alloc(n)
        q = WQUEUE[name]
        deps = list(wdeps) + [self.conv[(name, fb)]]
        d = self.dma(q, self.WT[:, off:off + n], self.wb16[name][fb], deps=deps)
        return off, d, seg

    def mm_group(self, off, ldep, KC, FBW, m, rhs_sb, rhs_dep, extra_mm=None, extra_deps=(), arena=None, mid_deps=None):
        wt = self.WT if arena is None else arena
        bank = self.banks.next()
        deps = [self.bfree.get(id(bank)), ldep, rhs_dep] + list(extra_deps)
        for k in range(KC):
            lh = wt[:, off + k * FBW + m * 128: off + k * FBW + (m + 1) * 128]
            rh = rhs_sb[:, k * GT:(k + 1) * GT]
            is_last = (k == KC - 1) and extra_mm is None
            kdeps = deps if k == 0 else ([mid_deps[k]] if mid_deps and k in mid_deps else [])
            d = self.op("pe", lambda e, b=bank, l=lh, r=rh, st=(k == 0), sp=is_last: e.matmul(
                b[:, :], l, r, start=st, stop=sp),
                deps=kdeps, inc=(self.sPE, 1) if is_last else None)
        if extra_mm is not None:
            d = self.op("pe", extra_mm(bank), deps=[], inc=(self.sPE, 1))
        return bank, d

    def stage_w1w2(self, rhs_sb, xb_dep, hooks=None):
        KC, FBW, NFB = WCFG["w1"]
        MPF = FBW // 128
        for fb in range(NFB):
            if hooks and fb in hooks:
                hooks[fb]()
            off1, l1, seg1 = self.load_wtile("w1", fb)
            off2, l2, seg2 = self.load_wtile("w2", fb)
            for m in range(MPF):
                gm = fb * MPF + m
                b1, d1 = self.mm_group(off1, l1, KC, FBW, m, rhs_sb, xb_dep)
                b2, d2 = self.mm_group(off2, l2, KC, FBW, m, rhs_sb, xb_dep)
                scr, wdep, sset = self.scr()
                da = self.op("act", lambda e, b=b1, o=scr, gm=gm: e.activation(
                    o[:, :], b[:, :], AF.Silu, scale=self.COLS["s1c"][:, gm:gm + 1]),
                    deps=[d1, wdep], inc=(self.sACT, 1))
                self.bfree[id(b1)] = da
                dd = self.op("dve", lambda e, b=b2, o=scr, gm=gm: e.scalar_tensor_tensor(
                    self.HIN[:, gm * GT:(gm + 1) * GT], b[:, :], self.COLS["s2c"][:, gm:gm + 1],
                    o[:, :], ALU.mult, ALU.mult),
                    deps=[d2, da, self.hin_free], inc=(self.sDVE, 1))
                self.bfree[id(b2)] = dd
                sset(dd)
            seg1[2] = (self.sPE, self.sPE.n)
            seg2[2] = (self.sPE, self.sPE.n)
        return (self.sDVE, self.sDVE.n)

    def stage_h(self, name, src, src_dep, dst, dst_free, scol, hooks=None):
        KC, FBW, NFB = WCFG[name]
        for fb in range(NFB):
            off, ld, seg = self.load_wtile(name, fb)
            for m in range(FBW // 128):
                gm = fb * (FBW // 128) + m
                bank, d = self.mm_group(off, ld, KC, FBW, m, src, src_dep)
                da = self.op("act", lambda e, b=bank, gm=gm, dst=dst: e.activation(
                    dst[:, gm * GT:(gm + 1) * GT], b[:, :], AF.Gelu, scale=self.COLS[scol][:, gm:gm + 1]),
                    deps=[d, dst_free], inc=(self.sACT, 1))
                self.bfree[id(bank)] = da
            seg[2] = (self.sPE, self.sPE.n)
            if hooks and fb in hooks:
                hooks[fb]()
        return (self.sACT, self.sACT.n), (self.sPE, self.sPE.n)

    def stage_w3(self, RES, pi, src, src_dep, with_stats, hooks=None):
        KC, FBW, NFB = WCFG["w3"]
        for fb in range(NFB):
            off, ld, seg = self.load_wtile("w3", fb)
            for m in range(FBW // 128):
                c = fb * (FBW // 128) + m
                bank, d = self.mm_group(off, ld, KC, FBW, m, src, src_dep)
                dd = self.op("dve", lambda e, b=bank, c=c, RES=RES: e.scalar_tensor_tensor(
                    RES[:, c * GT:(c + 1) * GT], b[:, :], self.S3H[:, c:c + 1],
                    RES[:, c * GT:(c + 1) * GT], ALU.mult, ALU.add),
                    deps=[d], inc=(self.sDVE, 1))
                self.bfree[id(bank)] = dd
                if with_stats:
                    self.stats_chunk(pi, RES, c, c == 0, dd)
            seg[2] = (self.sPE, self.sPE.n)
            if hooks and fb in hooks:
                hooks[fb]()
        return (self.sDVE, self.sDVE.n)

    def emit_prefetch(self, j):
        if j >= len(self.GROUPS):
            return
        kind, s, g = self.GROUPS[j]
        pi = j % 2
        t0 = s * SEQ + g * GT
        RES = self.RESP[pi]
        src = self.x0 if kind == "p1" else self.x1s
        ld = self.dma("act", RES[:, :].rearrange("p (c t) -> p c t", c=NCH),
                      src[:, t0:t0 + GT].rearrange("(c p) t -> p c t", p=128),
                      deps=self.res_free[pi])
        for c in range(NCH):
            self.stats_chunk(pi, RES, c, c == 0, ld)
        self.res_free[pi] = []
        if kind == "p2" and j > 0 and self.GROUPS[j - 1][0] == "p2":
            self.pre_qslab = self.dma("act", self.QB[:, :],
                                      self.qs[g * 128:(g + 1) * 128, s * SEQ:(s + 1) * SEQ],
                                      deps=[self.qperm_dep])

    def emit_lnpre(self, j):
        if j >= len(self.GROUPS):
            return
        kind, s, g = self.GROUPS[j]
        pi = j % 2
        if kind == "p1":
            rb, ab = self.ln_finish(pi, self.BCA, self.BCB, bf=True)
            xb = self.ln_apply(self.RESP[pi], self.BCA, self.BCB, rb, ab,
                               dst_dep_extra=self.po_free, dst=self.PO)
            self.pre = {"xb": xb}
        else:
            rb, ab = self.ln_finish(pi, self.KV0, self.KV1, bf=False)
            if self.GROUPS[j - 1][0] == "p2":
                qslab = self.pre_qslab
            else:
                qslab = self.dma("act", self.QB[:, :],
                                 self.qs[g * 128:(g + 1) * 128, s * SEQ:(s + 1) * SEQ],
                                 deps=[(self.sPE, self.sPE.n), (self.sDVE, self.sDVE.n)])
            self.pre = {"rb1": rb, "ab1": ab, "qslab": qslab}

    def p1_main(self, i, s, g):
        t0 = s * SEQ + g * GT
        pi = i % 2
        RES = self.RESP[pi]
        xb_dep = self.pre["xb"]
        self.banks.set_width(5)
        hin_dep = self.stage_w1w2(self.PO, xb_dep)
        self.po_free = (self.sPE, self.sPE.n)
        self.emit_prefetch(i + 1)
        hout_dep, hin_read = self.stage_h("h1", self.HIN, hin_dep, self.HOUT, self.hout_free, "sh1c")
        hin2_dep, hout_read = self.stage_h("h2", self.HOUT, hout_dep, self.HIN, hin_read, "sh2c")
        self.hout_free = hout_read
        x1_dep = self.stage_w3(RES, pi, self.HIN, hin2_dep, with_stats=True)
        self.hin_free = (self.sPE, self.sPE.n)
        self.banks.set_width(3)
        st = self.dma("act", self.x1s[:, t0:t0 + GT].rearrange("(c p) t -> p c t", p=128),
                      RES[:, :].rearrange("p (c t) -> p c t", c=NCH), deps=[x1_dep])
        f1 = self.ln_finish(pi, self.BCA, self.BCB, bf=True, defer=True)
        if self.kv_pend is not None:
            fn, p = self.kv_pend
            self.kv_pend = None
            fn(*p)
            self.vt_free = (self.sPE, self.sPE.n)
        rb, ab = f1()
        xcd = self.ln_apply_cd(RES, self.BCA, self.BCB, rb, ab,
                               dst_dep_extra=(self.sPE, self.sPE.n))
        xb1_dep = xcd[-1]
        q_last = None
        for fb in range(2):
            off, ld, seg = self.load_wtile("wq", fb)
            for m in range(4):
                c = fb * 4 + m
                bank, d = self.mm_group(off, ld, 8, 512, m, self.XB, xcd[0],
                                        mid_deps={k: xcd[k] for k in range(1, 8)})
                scr0, w0, s0 = self.scr()
                scr1, w1, s1 = self.scr()
                dm = self.op("dve", lambda e, b=bank, o=scr0, c=c: e.tensor_scalar(
                    o[:, :], b[:, :], self.COLS["bqc"][:, c:c + 1], 0.0, ALU.add, ALU.min),
                    deps=[d, w0, self.qb_store], inc=(self.sDVE, 1))
                du = self.op("act", lambda e, b=bank, o=scr1, c=c: e.activation(
                    o[:, :], b[:, :], AF.Identity, bias=self.BQ1[:, c:c + 1]),
                    deps=[d, w1, self.bq1_dep, dm], inc=(self.sACT, 1))
                dx = self.op("act", lambda e, o=scr0: e.activation(o[:, :], o[:, :], AF.Exp),
                             deps=[dm], inc=(self.sACT, 1))
                dq = self.op("dve", lambda e, a=scr0, b2=scr1, c=c: e.tensor_max(
                    self.QB[:, c * GT:(c + 1) * GT], a[:, :], b2[:, :]),
                    deps=[du, dx], inc=(self.sDVE, 1))
                s0(dq)
                s1(dq)
                self.bfree[id(bank)] = dx
                q_last = dq
            seg[2] = (self.sPE, self.sPE.n)
        self.qb_store = self.dma("act", self.qs[:, t0:t0 + GT].rearrange("(c p) t -> p c t", p=128),
                                 self.QB[:, :].rearrange("p (c t) -> p c t", c=NCH), deps=[q_last])
        first = (g == 0)
        last = (g == NGS - 1)
        do_lnpre = i + 1 < len(self.GROUPS) and self.GROUPS[i + 1][0] == "p1"
        pi2 = (i + 1) % 2
        lnv = {}
        vsec = 0

        def lnpre_hook():
            if not do_lnpre:
                return
            if vsec == 2:
                lnv["f"] = self.ln_finish(pi2, self.BCA, self.BCB, bf=True, defer=True)
            elif vsec == 4:
                lnv["rbab"] = lnv["f"]()
            elif vsec in (5, 6, 7, 8):
                rb, ab = lnv["rbab"]
                last = None
                for c in range(2 * (vsec - 5), 2 * (vsec - 5) + 2):
                    last = self.ln_apply_chunk(self.RESP[pi2], self.BCA, self.BCB, rb, ab,
                                               c, self.PO, dst_dep_extra=self.po_free)
                if vsec == 8:
                    self.bfree[id(self.BCA)] = last
                    self.bfree[id(self.BCB)] = last
                    self.pre = {"xb": last}
        for fs in (2, 3):
            off, ld, seg = self.load_wtile("kv", fs)
            for tc in range(TC):
                bank = self.banks.next()
                deps = [self.bfree.get(id(bank)), ld, xb1_dep, self.vt_free]
                for k in range(8):
                    lh = self.XB[:, k * GT + tc * 128: k * GT + (tc + 1) * 128]
                    rh = self.WT[:, off + k * 512:off + (k + 1) * 512]
                    self.op("pe", lambda e, b=bank, l=lh, r=rh, st=(k == 0): e.matmul(
                        b[:, :], l, r, start=st, stop=False),
                        deps=deps if k == 0 else [])
                d = self.op("pe", lambda e, b=bank, fs=fs: e.matmul(
                    b[:, :], self.ONEB[0:1, 0:128], self.BKV[0:1, fs * 512:(fs + 1) * 512],
                    start=False, stop=True), deps=[], inc=(self.sPE, 1))
                dv = self.op("act", lambda e, b=bank, fs=fs, tc=tc: e.activation(
                    self.VT[:, tc * 1024 + (fs - 2) * 512: tc * 1024 + (fs - 1) * 512],
                    b[:, :], AF.Copy), deps=[d], inc=(self.sACT, 1))
                self.bfree[id(bank)] = dv
                vsec += 1
                lnpre_hook()
            seg[2] = (self.sPE, self.sPE.n)
        vt_ready = (self.sACT, self.sACT.n)
        offk = []
        for fs in (0, 1):
            off, ld, seg = self.load_wtile("kv", fs)
            offk.append((off, ld, seg))
        kt_free = [None, None]
        pend = None

        def emit_acc(tc, evac_deps):
            ko = (tc % 2) * 1024
            kdeps = evac_deps + [self.kv_bank_free, vt_ready]
            for pr in range(8):
                kvb = self.KV0 if pr < 4 else self.KV1
                jj = (pr % 4) * 128
                self.op("pe", lambda e, b=kvb, pr=pr, jj=jj, tc=tc, ko=ko,
                        st=(first and tc == 0 and pr % 4 == 0), sp=last: e.matmul(
                    b[:, jj:jj + 128], self.KT[:, ko + pr * 128:ko + (pr + 1) * 128],
                    self.VT[:, tc * 1024 + pr * 128: tc * 1024 + (pr + 1) * 128],
                    start=st, stop=sp, skip_group_check=True),
                    deps=kdeps if pr == 0 else [])
            dd = None
            for pr in range(8):
                dd = self.op("pe", lambda e, pr=pr, ko=ko, st=(first and tc == 0 and pr == 0), sp=last: e.matmul(
                    self.K1P[:, pr:pr + 1], self.KT[:, ko + pr * 128:ko + (pr + 1) * 128],
                    self.ONECB[:, :], start=st, stop=sp, skip_group_check=True),
                    deps=[self.ones_dep, self.kv_bank_free] if pr == 0 else [],
                    inc=(self.sPE, 1) if pr == 7 else None)
            kt_free[tc % 2] = dd

        for tc in range(TC):
            evac_deps = []
            ko = (tc % 2) * 1024
            for fi, (off, ld, seg) in enumerate(offk):
                bank = self.banks.next()
                deps = [self.bfree.get(id(bank)), ld, xb1_dep]
                for k in range(8):
                    lh = self.XB[:, k * GT + tc * 128: k * GT + (tc + 1) * 128]
                    rh = self.WT[:, off + k * 512:off + (k + 1) * 512]
                    self.op("pe", lambda e, b=bank, l=lh, r=rh, st=(k == 0): e.matmul(
                        b[:, :], l, r, start=st, stop=False),
                        deps=deps if k == 0 else [])
                d = self.op("pe", lambda e, b=bank, fs=fi: e.matmul(
                    b[:, :], self.ONEB[0:1, 0:128], self.BKV[0:1, fs * 512:(fs + 1) * 512],
                    start=False, stop=True), deps=[], inc=(self.sPE, 1))
                scr0, w0, s0 = self.scr()
                scr1, w1, s1 = self.scr()
                dm = self.op("dve", lambda e, b=bank, o=scr0: e.tensor_scalar(
                    o[:, :], b[:, :], 0.0, None, ALU.min), deps=[d, w0], inc=(self.sDVE, 1))
                du = self.op("act", lambda e, b=bank, o=scr1: e.activation(
                    o[:, :], b[:, :], AF.Copy, bias=1.0), deps=[d, w1, dm], inc=(self.sACT, 1))
                dx = self.op("act", lambda e, o=scr0: e.activation(o[:, :], o[:, :], AF.Exp),
                             deps=[dm], inc=(self.sACT, 1))
                dk = self.op("dve", lambda e, a=scr0, b2=scr1, fs=fi, ko=ko: e.tensor_max(
                    self.KT[:, ko + fs * 512:ko + (fs + 1) * 512], a[:, :], b2[:, :]),
                    deps=[du, dx, kt_free[tc % 2]], inc=(self.sDVE, 1))
                s0(dk)
                s1(dk)
                self.bfree[id(bank)] = dx
                evac_deps.append(dk)
            if pend is not None:
                emit_acc(*pend)
            pend = (tc, evac_deps)
        if last:
            emit_acc(*pend)
        else:
            self.kv_pend = (emit_acc, pend)
        for off, ld, seg in offk:
            seg[2] = (self.sPE, self.sPE.n)
        self.vt_free = (self.sPE, self.sPE.n)
        self.res_free[pi] = [(self.sDVE, self.sDVE.n), (self.sACT, self.sACT.n)]

    def kv_evac(self, s):
        kv_done = (self.sPE, self.sPE.n)
        self.op("dve", lambda e, s=s: e.memset(self.K1M2[:, s * 264:(s + 1) * 264], 0.0),
                deps=[kv_done])
        for pr in range(8):
            self.op("dve", lambda e, s=s, pr=pr: e.tensor_copy(
                self.KVE[0:64, s * 512 + pr * 64: s * 512 + (pr + 1) * 64],
                (self.KV0 if pr < 4 else self.KV1)[0:64, (pr % 4) * 128:(pr % 4) * 128 + 64]),
                deps=[])
            self.op("dve", lambda e, s=s, pr=pr: e.tensor_copy(
                self.KVE[64:128, s * 512 + pr * 64: s * 512 + (pr + 1) * 64],
                (self.KV0 if pr < 4 else self.KV1)[64:128, (pr % 4) * 128 + 64:(pr % 4) * 128 + 128]),
                deps=[])
            base = s * 264 + pr * 33
            self.op("dve", lambda e, base=base, pr=pr: e.tensor_copy(
                self.K1M2[0:64, base: base + 1], self.K1P[0:64, pr:pr + 1]), deps=[])
            inc = (self.sDVE, 1) if pr == 7 else None
            self.kv_bank_free = self.op("dve", lambda e, base=base, pr=pr: e.tensor_copy(
                self.K1M2[64:128, base + 32: base + 33], self.K1P[64:128, pr:pr + 1]),
                deps=[], inc=inc)
        self.kve_dep = self.kv_bank_free

    def p2_island_setup(self, j, qslab):
        kind, s, g = self.GROUPS[j]
        QP = self.PO
        qperm = self.op("dve", lambda e: e.tensor_copy(
            QP[:, :].rearrange("p (j m) -> p j m", j=16),
            self.QB[:, :].rearrange("p (m j) -> p j m", j=16)),
            deps=[qslab, self.qp2_free, self.po_free], inc=(self.sDVE, 1))
        self.qperm_dep = qperm
        for fb in range(2):
            ld = self.dma("sp", self.WPT[:, fb * 4096:(fb + 1) * 4096],
                          self.wb16["wp"][fb], deps=[self.wp_free, self.conv[("wp", fb)]])
            self.wp_ld[fb] = ld
        self.isl = {"j": j, "s": s, "g": g, "QP": QP, "qperm": qperm, "xb_dep": None}

    def p2_island_cc_a(self, cc):
        isl = self.isl
        s, g, QP, qperm = isl["s"], isl["g"], isl["QP"], isl["qperm"]
        qk_deps = []
        for pj in range(2):
            j0 = 2 * cc + pj
            dqk = self.op("pe", lambda e, s=s, g=g, pj=pj, j0=j0, QP=QP: e.matmul(
                self.K1P[0:33, pj * 256:(pj + 1) * 256],
                self.K1M2[:, s * 264 + g * 33: s * 264 + (g + 1) * 33],
                QP[:, j0 * 256:(j0 + 1) * 256], start=True, stop=True, skip_group_check=True),
                deps=[qperm, self.kve_dep, self.kv_bank_free, self.k1p_free], inc=(self.sPE, 1))
            dmx = self.op("dve", lambda e, pj=pj: e.tensor_scalar(
                self.REC33[:, pj * 256:(pj + 1) * 256], self.K1P[0:33, pj * 256:(pj + 1) * 256],
                1e-6, None, ALU.max), deps=[dqk])
            def _recip(e, pj=pj):
                with e.bass.allow_low_precision(reason="bf16 reciprocal feeds bf16 attn matmuls"):
                    return e.reciprocal(self.REC33[:, pj * 256:(pj + 1) * 256],
                                        self.REC33[:, pj * 256:(pj + 1) * 256])
            drc = self.op("dve", _recip, deps=[], inc=(self.sDVE, 1))
            qk_deps.append(drc)
        self.k1p_free = qk_deps[-1]
        isl["qk_deps"] = qk_deps

    def p2_island_cc_b(self, cc):
        isl = self.isl
        s, g, QP = isl["s"], isl["g"], isl["QP"]
        qk_deps = isl["qk_deps"]
        rb_b = self.banks.next()
        nb = self.banks.next()
        last_mm = None
        for pj in range(2):
            for th in range(2):
                row = 0 if th == 0 else 32
                self.op("pe", lambda e, b=rb_b, pj=pj, th=th, row=row: e.matmul(
                    b[pj * 64:(pj + 1) * 64, th * 256:(th + 1) * 256],
                    self.ONE33[row:row + 1, 0:64],
                    self.REC33[row:row + 1, pj * 256:(pj + 1) * 256],
                    start=True, stop=True, skip_group_check=True),
                    deps=[qk_deps[pj], self.bfree.get(id(rb_b))])
                last_mm = self.op("pe", lambda e, b=nb, s=s, g=g, pj=pj, th=th, j0=2 * cc + pj, QP=QP: e.matmul(
                    b[pj * 64:(pj + 1) * 64, th * 256:(th + 1) * 256],
                    self.KVE[th * 64:(th + 1) * 64, s * 512 + g * 64:s * 512 + (g + 1) * 64],
                    QP[th * 64:(th + 1) * 64, j0 * 256:(j0 + 1) * 256],
                    start=True, stop=True, skip_group_check=True),
                    deps=[self.bfree.get(id(nb))],
                    inc=(self.sPE, 1) if (pj == 1 and th == 1) else None)
        dall = last_mm
        dc = self.op("act", lambda e, b=rb_b: e.activation(self.REC[:, :], b[:, :], AF.Copy),
                     deps=[dall, self.rec_free], inc=(self.sACT, 1))
        self.bfree[id(rb_b)] = dc
        dx = self.op("dve", lambda e, b=nb, cc=cc: e.tensor_mul(
            self.XB[:, cc * GT:(cc + 1) * GT], b[:, :], self.REC[:, :]),
            deps=[dall, dc], inc=(self.sDVE, 1))
        self.bfree[id(nb)] = dx
        self.rec_free = dx
        isl["xb_dep"] = dx

    def p2_island_tail_proj(self):
        isl = self.isl
        pi = isl["j"] % 2
        xb_attn_dep = isl["xb_dep"]
        self.qp2_free = (self.sPE, self.sPE.n)
        for fb in range(2):
            off, ld = fb * 4096, self.wp_ld[fb]
            for m in range(4):
                c = fb * 4 + m
                bank, d = self.mm_group(off, ld, 8, 512, m, self.XB, xb_attn_dep,
                                        arena=self.WPT)
                dp = self.op("act", lambda e, b=bank, c=c: e.activation(
                    self.PO[:, c * GT:(c + 1) * GT], b[:, :], AF.Identity,
                    bias=self.COLS["bpc"][:, c:c + 1]), deps=[d, self.po_free], inc=(self.sACT, 1))
                self.bfree[id(bank)] = dp
                self.stats_chunk(pi, self.PO, c, c == 0, dp)
        self.wp_free = (self.sPE, self.sPE.n)

    def p2_island_tail_x2_chunk(self, c, rb2, ab2):
        isl = self.isl
        j = isl["j"]
        pi = j % 2
        RES = self.RESP[pi]
        rb1 = self.pre["rb1"]
        ab1 = self.pre["ab1"]
        scr0, w0, s0 = self.scr()
        scr1, w1, s1 = self.scr()
        scr2, w2, s2 = self.scr()
        self.op("dve", lambda e, c=c, o=scr0: e.tensor_mul(
            o[:, :], self.PO[:, c * GT:(c + 1) * GT], self.BCA[:, :]), deps=[rb2, w0])
        dz0 = self.op("dve", lambda e, o=scr0: e.tensor_add(o[:, :], o[:, :], self.BCB[:, :]),
                      deps=[ab2], inc=(self.sDVE, 1))
        dz = self.op("act", lambda e, o=scr0, z=scr1, c=c: e.activation(
            z[:, :], o[:, :], AF.Identity, scale=self.COLS["gamc"][:, c:c + 1],
            bias=self.COLS["betc"][:, c:c + 1]), deps=[dz0, w1], inc=(self.sACT, 1))
        s0(dz)
        self.op("dve", lambda e, o=scr2, c=c, RES=RES: e.tensor_mul(
            o[:, :], RES[:, c * GT:(c + 1) * GT], self.KV0[:, :]), deps=[rb1, w2])
        dyv = self.op("dve", lambda e, o=scr2: e.tensor_add(o[:, :], o[:, :], self.KV1[:, :]),
                      deps=[ab1], inc=(self.sDVE, 1))
        self.op("dve", lambda e, z=scr1, o=scr2: e.tensor_add(z[:, :], z[:, :], o[:, :]),
                deps=[dz, dyv])
        dx2 = self.op("dve", lambda e, z=scr1, c=c, RES=RES: e.tensor_add(
            RES[:, c * GT:(c + 1) * GT], z[:, :], RES[:, c * GT:(c + 1) * GT]),
            deps=[], inc=(self.sDVE, 1))
        s1(dx2)
        s2(dx2)
        self.stats_chunk(pi, RES, c, c == 0, dx2)
        if c == NCH - 1:
            x2_done = (self.sDVE, self.sDVE.n)
            self.bfree[id(self.BCA)] = x2_done
            self.bfree[id(self.BCB)] = x2_done
            self.bfree[id(self.KV0)] = x2_done
            self.bfree[id(self.KV1)] = x2_done
            self.po_free = x2_done
            self.island_j = j

    def p2_island_tail_x2(self, rb2, ab2):
        for c in range(NCH):
            self.p2_island_tail_x2_chunk(c, rb2, ab2)

    def p2_island(self, j):
        self.p2_island_setup(j, self.pre["qslab"])
        for cc in range(NCH):
            self.p2_island_cc_a(cc)
            self.p2_island_cc_b(cc)
        self.p2_island_tail_proj()
        rb2, ab2 = self.ln_finish(j % 2, self.BCA, self.BCB)
        self.p2_island_tail_x2(rb2, ab2)

    def p2_island_b_part1(self):
        return self.ln_finish(self.island_j % 2, self.BCA, self.BCB, bf=True, defer=True)

    def p2_island_b_part2(self, fn):
        j = self.island_j
        pi = j % 2
        RES = self.RESP[pi]
        rb3, ab3 = fn()
        self.pre2 = self.ln_apply(RES, self.BCA, self.BCB, rb3, ab3,
                                  dst_dep_extra=(self.sPE, self.sPE.n))

    def p2_island_b(self):
        j = self.island_j
        pi = j % 2
        RES = self.RESP[pi]
        rb3, ab3 = self.ln_finish(pi, self.BCA, self.BCB, bf=True)
        self.pre2 = self.ln_apply(RES, self.BCA, self.BCB, rb3, ab3,
                                  dst_dep_extra=(self.sPE, self.sPE.n))

    def p2_main(self, i, s, g):
        t0 = s * SEQ + g * GT
        pi = i % 2
        RES = self.RESP[pi]
        xb2_dep = self.pre2
        self.banks.set_width(5)
        hin_dep = self.stage_w1w2(self.XB, xb2_dep)
        self.emit_prefetch(i + 1)
        hooks = None
        if i + 1 < len(self.GROUPS):
            self.p2_island_setup(i + 1, self.pre_qslab)

            def hookfn(k):
                def h():
                    if k > 0:
                        self.p2_island_cc_b(k - 1)
                    if k < NCH:
                        self.p2_island_cc_a(k)
                return h
            hooks = {12 + k: hookfn(k) for k in range(NCH + 1)}
            pi2 = (i + 1) % 2
            lnp = {}

            def lnpre1():
                lnp["f"] = self.ln_finish(pi2, self.KV0, self.KV1, defer=True)

            def lnpre2():
                rb1, ab1 = lnp["f"]()
                self.pre = {"rb1": rb1, "ab1": ab1}
            hooks[22] = lnpre1
            hooks[24] = lnpre2
        hout_dep, hin_read = self.stage_h("h1", self.HIN, hin_dep, self.HOUT, self.hout_free, "sh1c", hooks=hooks)
        self.banks.set_width(3)
        h2_hooks = None
        if i + 1 < len(self.GROUPS):
            self.p2_island_tail_proj()
            lnq = {}
            h2_hooks = {
                0: lambda: lnq.update(f=self.ln_finish((i + 1) % 2, self.BCA, self.BCB, defer=True)),
                1: lambda: lnq.update(rbab=lnq["f"]()),
            }

            def x2h(c):
                return lambda: self.p2_island_tail_x2_chunk(c, *lnq["rbab"])
            for c in range(NCH):
                h2_hooks[2 + c] = x2h(c)
        hin2_dep, hout_read = self.stage_h("h2", self.HOUT, hout_dep, self.HIN, hin_read, "sh2c", hooks=h2_hooks)
        self.hout_free = hout_read
        w3_hooks = None
        if i + 1 < len(self.GROUPS):
            ib = {}
            w3_hooks = {1: lambda: ib.update(f=self.p2_island_b_part1()),
                        2: lambda: self.p2_island_b_part2(ib["f"])}
        x3_dep = self.stage_w3(RES, pi, self.HIN, hin2_dep, with_stats=False, hooks=w3_hooks)
        self.hin_free = (self.sPE, self.sPE.n)
        self.dma("act", self.y[:, t0:t0 + GT].rearrange("(c p) t -> p c t", p=128),
                 RES[:, :].rearrange("p (c t) -> p c t", c=NCH), deps=[x3_dep])
        self.res_free[pi] = [x3_dep]


def _tile_w(w, KC, FBW):
    out_f, in_f = w.shape
    NFB = out_f // FBW
    wt = np.ascontiguousarray(w.T)
    arr = wt.reshape(KC, 128, NFB, FBW).transpose(2, 1, 0, 3)
    return np.ascontiguousarray(arr.reshape(NFB, 128, KC * FBW))


def _col(s, n):
    return np.ascontiguousarray(np.asarray(s, np.float32).reshape(n, 128).T)


def kernel(L0_attn_H, L0_attn_W_qkv, L0_attn_b_qkv, L0_attn_W_proj, L0_attn_b_proj,
           L0_attn_gamma, L0_attn_beta, L0_ffn_w1_w, L0_ffn_w1_s, L0_ffn_w2_w,
           L0_ffn_w2_s, L0_ffn_h1_w, L0_ffn_h1_s, L0_ffn_h2_w, L0_ffn_h2_s,
           L0_ffn_w3_w, L0_ffn_w3_s):
    f = lambda a: np.asarray(a, np.float32)
    H = f(L0_attn_H)
    Wqkv = f(L0_attn_W_qkv)

    weights = {
        "w1t": _tile_w(f(L0_ffn_w1_w), *WCFG["w1"][:2]),
        "w2t": _tile_w(f(L0_ffn_w2_w), *WCFG["w2"][:2]),
        "h1t": _tile_w(f(L0_ffn_h1_w), *WCFG["h1"][:2]),
        "h2t": _tile_w(f(L0_ffn_h2_w), *WCFG["h2"][:2]),
        "w3t": _tile_w(f(L0_ffn_w3_w), *WCFG["w3"][:2]),
        "kvt": _tile_w(Wqkv[1024:3072], *WCFG["kv"][:2]),
        "wqt": _tile_w(Wqkv[0:1024], *WCFG["wq"][:2]),
        "wpt": _tile_w(f(L0_attn_W_proj), *WCFG["wp"][:2]),
        "bkv": np.ascontiguousarray(f(L0_attn_b_qkv)[1024:3072].reshape(1, 2048)),
        "s1c": _col(L0_ffn_w1_s, 32), "s2c": _col(L0_ffn_w2_s, 32),
        "sh1c": _col(L0_ffn_h1_s, 32), "sh2c": _col(L0_ffn_h2_s, 32),
        "s3c": _col(L0_ffn_w3_s, 8),
        "gamc": _col(L0_attn_gamma, 8), "betc": _col(L0_attn_beta, 8),
        "bqc": _col(f(L0_attn_b_qkv)[0:1024], 8),
        "bpc": _col(L0_attn_b_proj, 8),
    }
    in_maps = []
    for i in range(NCORES):
        xs = np.ascontiguousarray(H[SPC * i:SPC * (i + 1)].reshape(SPC * SEQ, D).T)
        m = {"x0": xs}
        m.update(weights)
        in_maps.append(m)

    mk = MK()
    nc = mk.build()
    res = run_bass_kernel_spmd(nc, in_maps, core_ids=list(range(NCORES)))
    globals()["_LAST"] = res
    outs = []
    for i in range(NCORES):
        yf = res.results[i]["y"]
        outs.append(np.ascontiguousarray(yf.T).reshape(SPC, SEQ, D))
    return np.concatenate(outs, axis=0)


if __name__ == "__main__":
    mk = MK()
    nc = mk.build()
    print("built n_inst =", nc.n_instructions())
